# version 11
# baseline (speedup 1.0000x reference)
import math
from contextlib import ExitStack

import numpy as np
import concourse.bass as bass
import concourse.mybir as mybir
from concourse.bass_utils import run_bass_kernel_spmd

F32 = mybir.dt.float32
BF16 = mybir.dt.bfloat16
I32 = mybir.dt.int32
AF = mybir.ActivationFunctionType
OP = mybir.AluOpType
AX = mybir.AxisListType

NCORES = 8
B, T, D = 8, 2048, 1024
COLS = T // NCORES
TOK = B * COLS
NTOK = B * T
DC = D // 128
HEADS, HDIM = 16, 64
QSCALE = 1.0 / math.sqrt(HDIM)
DEQ = 1.0 / 127.0
EPS = 1e-5
NEG = -1e30
ALIBI = [2.0 ** (-0.5 * (i + 1)) for i in range(HEADS)]
MOE_E, MOE_I, S_I = 64, 704, 1408


class Prog:

    ENGINES = ("sync", "tensor", "vector", "scalar", "gpsimd")

    def __init__(self, nc, ctx):
        self.nc = nc
        self.ctx = ctx
        self.ops = {e: [] for e in self.ENGINES}
        self._sems = {}
        self._barsc = ctx.enter_context(
            nc.sbuf_tensor("barsc", [128, 4], mybir.dt.float32))

    def _s(self, name):
        if name not in self._sems:
            pool = getattr(self, "_pool", None)
            if name in getattr(self, "fresh_names", ()):
                pool = None
            if pool:
                h, cnt = pool.pop()
                self._sems[name] = [h, cnt, cnt]
            else:
                h = self.ctx.enter_context(self.nc.semaphore(name))
                self._sems[name] = [h, 0, 0]
        return self._sems[name]

    def rel(self, name):
        s = self._s(name)
        return s[1] - s[2]

    def release_sems(self, keep=()):
        if not hasattr(self, "_pool"):
            self._pool = []
        for name in list(self._sems):
            if name.startswith("bar_") or name in keep:
                continue
            s = self._sems.pop(name)
            self._pool.append((s[0], s[1]))

    def inc(self, name, n=1):
        s = self._s(name)
        s[1] += n
        return s[0], s[1]

    def val(self, name):
        return self._s(name)[1]

    def rt_chain(self, eng, inst):
        if inst is None:
            return None
        key = str(getattr(eng, "engine", getattr(eng, "engine_type", type(eng).__name__)))
        if not hasattr(self, "_chain"):
            self._chain = {}
        if key not in self._chain:
            self._chain[key] = [self.ctx.enter_context(self.nc.semaphore(f"chain_{len(self._chain)}")), 0]
        c = self._chain[key]
        c[1] += 1
        inst.then_inc(c[0], 1)
        eng.wait_ge(c[0], c[1])
        return inst

    def emit(self, engine, fn):
        d = getattr(self, "_defer", None)
        if d is not None:
            d[engine].append(fn)
        else:
            self.ops[engine].append(fn)

    def defer_start(self):
        self._defer = {e: [] for e in self.ENGINES}

    def defer_end(self):
        d = self._defer
        self._defer = None
        return d

    def emit_deferred(self, d):
        if d:
            for e, fns in d.items():
                self.ops[e].extend(fns)

    def wait(self, engine, name, val=None):
        s = self._s(name)
        v = s[1] if val is None else s[2] + val
        if v <= 0:
            return
        self.emit(engine, lambda e, h=s[0], v=v: e.wait_ge(h, v))

    def barrier(self, engines=None):
        engines = list(engines or self.ENGINES)
        targets = {}
        col = {"vector": 0, "scalar": 1, "gpsimd": 2}
        for e in engines:
            h, v = self.inc(f"bar_{e}", 1)
            targets[e] = (h, v)
            if e == "vector":
                self.emit(e, lambda eng, h=h, v=v: (eng.memset(
                    self._barsc[:, 0:1], 0.0).then_inc(h, 1), eng.wait_ge(h, v)))
            elif e == "gpsimd":
                self.emit(e, lambda eng, h=h, v=v: (eng.memset(
                    self._barsc[:, 2:3], 0.0).then_inc(h, 1), eng.wait_ge(h, v)))
            elif e == "scalar":
                self.emit(e, lambda eng, h=h, v=v: (eng.memzero(
                    self._barsc[:, 1:2]).then_inc(h, 1), eng.wait_ge(h, v)))
            else:
                self.emit(e, lambda eng, h=h: eng.nop().then_inc(h, 1))
        for e in engines:
            for o in engines:
                if o != e:
                    h, v = targets[o]
                    self.emit(e, lambda eng, h=h, v=v: eng.wait_ge(h, v))

    def build(self, block):
        for e in self.ENGINES:
            ops = self.ops[e]
            getattr(block, e)(lambda eng, ops=ops: [op(eng) for op in ops])


def sb(ctx, nc, name, shape, dtype, side=None):
    return ctx.enter_context(nc.sbuf_tensor(name, shape, dtype, side=side))


def ps(ctx, nc, name, shape, dtype=F32):
    return ctx.enter_context(nc.psum_tensor(name, shape, dtype))


def emit_ln_tile(p, x_ap, out_ap, sc, done_sem, wait_before=None):
    s_sq, _ = p.inc(done_sem + "_sq", 1)
    v_sq = p.rel(done_sem + "_sq")
    if wait_before:
        p.wait("scalar", *wait_before)
    p.emit("scalar", lambda s, x=x_ap: s.activation(
        sc["sq"][:, :], x, AF.Square, accum_out=sc["sqs"][:, :]).then_inc(s_sq, 1))
    p.wait("vector", done_sem + "_sq", v_sq)
    s_st, _ = p.inc(done_sem + "_st", 1)
    v_st = p.rel(done_sem + "_st")

    def v_stats(v, x=x_ap):
        c0 = p.rt_chain(v, None)
        c0 = p.rt_chain(v, v.tensor_reduce(sc["ssum"][:, :], x, AX.X, OP.add))
        c0 = p.rt_chain(v, v.tensor_scalar(sc["mean"][:, :], sc["ssum"][:, :], 1.0 / D, None, OP.mult))
        c0 = p.rt_chain(v, v.tensor_tensor(sc["var"][:, :], sc["mean"][:, :], sc["mean"][:, :], OP.mult))
        c0 = p.rt_chain(v, v.tensor_scalar(sc["sqs"][:, :], sc["sqs"][:, :], 1.0 / D, None, OP.mult))
        c0 = p.rt_chain(v, v.tensor_tensor(sc["var"][:, :], sc["sqs"][:, :], sc["var"][:, :], OP.subtract))
        v.tensor_scalar(sc["var"][:, :], sc["var"][:, :], EPS, None, OP.add).then_inc(s_st, 1)
    p.emit("vector", v_stats)
    p.wait("scalar", done_sem + "_st", v_st)
    s_rs, _ = p.inc(done_sem + "_rs", 1)
    v_rs = p.rel(done_sem + "_rs")

    def s_rstd(s):
        p.rt_chain(s, s.activation(sc["var"][:, :], sc["var"][:, :], AF.Ln))
        s.activation(sc["rstd"][:, :], sc["var"][:, :], AF.Exp, scale=-0.5).then_inc(s_rs, 1)
    p.emit("scalar", s_rstd)
    p.wait("vector", done_sem + "_rs", v_rs)
    hd, _ = p.inc(done_sem, 1)
    p.emit("vector", lambda v, x=x_ap, o=out_ap: v.tensor_scalar(
        o, x, sc["mean"][:, :], sc["rstd"][:, :], OP.subtract, OP.mult).then_inc(hd, 1))


def make_ln_scratch(ctx, nc, prefix):
    sc = {"sq": sb(ctx, nc, prefix + "sq", [128, 1024], F32)}
    for n in ("ssum", "mean", "sqs", "var", "rstd"):
        sc[n] = sb(ctx, nc, prefix + n, [128, 1], F32)
    return sc



def _emit_dummy(p, nc, d_out):
    with nc.sbuf_tensor("zz", [128, 16, 1024], F32) as zz:
        hz, _ = p.inc("zz", 1)
        p.emit("gpsimd", lambda g, hz=hz: g.memset(zz[:, :, :], 0.0).then_inc(hz, 1))
        p.wait("sync", "zz")
        hd, _ = p.inc("d_x1d", 16)
        p.emit("sync", lambda s, hd=hd: s.dma_start(
            out=d_out.ap().rearrange("(c p) f -> p c f", p=128), in_=zz[:, :, :]).then_inc(hd, 16))
        p.wait("sync", "d_x1d")


def _l0_drams(nc, x1_kind="ExternalOutput", kv_kind="Internal"):
    dd = {}
    dd["xf"] = nc.dram_tensor("x_full", [NTOK, D], F32, kind="ExternalInput")
    dd["xo"] = nc.dram_tensor("x_own", [TOK, D], F32, kind="ExternalInput")
    dd["thr"] = nc.dram_tensor("thr", [1, 1], F32, kind="ExternalInput")
    dd["wattnT"] = nc.dram_tensor("w_attnT", [D, 3 * D], F32, kind="ExternalInput")
    dd["wprojT"] = nc.dram_tensor("w_projT", [D, D], F32, kind="ExternalInput")
    dd["wg0T"] = nc.dram_tensor("wg0T", [D, 3072], I32, kind="ExternalInput")
    dd["wu0T"] = nc.dram_tensor("wu0T", [D, 3072], I32, kind="ExternalInput")
    dd["wd0T"] = nc.dram_tensor("wd0T", [3072, D], I32, kind="ExternalInput")
    dd["x1"] = nc.dram_tensor("x1_own", [TOK, D], F32, kind=x1_kind)
    dd["ksc"] = nc.dram_tensor("Ksc", [D, NTOK], BF16, kind=kv_kind)
    dd["vsc"] = nc.dram_tensor("Vsc", [D, NTOK], BF16, kind=kv_kind)
    return dd


def build_l0(debug_kv=False, upto="full", skip_a=False, kv_ext=False, nch=32):
    nc = bass.Bass("TRN2", target_bir_lowering=False)
    ctx = ExitStack()
    nc._ctx_keepalive = ctx
    dd = _l0_drams(nc, kv_kind="ExternalOutput" if (debug_kv or kv_ext) else "Internal")
    p = Prog(nc, ctx)
    t = {}
    emit_l0_body(p, ctx, nc, t, dd, debug_kv, upto, skip_a, nch)
    with nc.Block() as block:
        p.build(block)
    return nc


def emit_l0_body(p, ctx, nc, t, dd, debug_kv=False, upto="full", skip_a=False, nch=32,
                 perm_ctx=None):
    d_xf, d_xo, d_thr = dd["xf"], dd["xo"], dd["thr"]
    d_wattnT, d_wprojT = dd["wattnT"], dd["wprojT"]
    d_wg0T, d_wu0T, d_wd0T = dd["wg0T"], dd["wu0T"], dd["wd0T"]
    d_x1, d_ksc, d_vsc = dd["x1"], dd["ksc"], dd["vsc"]

    pctx = perm_ctx if perm_ctx is not None else ctx
    t["ident"] = sb(pctx, nc, "ident", [128, 128], BF16)
    t["ones1"] = sb(pctx, nc, "ones1", [1, 128], BF16)
    t["ones1f"] = sb(pctx, nc, "ones1f", [1, 128], F32)
    t["thr1"] = sb(ctx, nc, "thr1", [1, 1], F32)
    t["thrc"] = sb(ctx, nc, "thrc", [128, 1], F32)
    era1 = ExitStack()
    t["kcol"] = sb(era1, nc, "kcol", [128, T], F32)
    t["mask"] = sb(era1, nc, "mask", [128, 2, T], BF16)
    t["wq"] = sb(era1, nc, "wq", [128, DC, D], BF16)

    with nc.sbuf_tensor("dmp", [128, T], F32) as dmp, \
         nc.psum_tensor("ps_i", [128, 1], F32) as ps_i:
        hg, _ = p.inc("g_ini", 1)

        def g_init(g, hg=hg):
            g.iota(dmp[:, :], [[1, T]], base=0, channel_multiplier=-1,
                   allow_small_or_imprecise_dtypes=True)
            g.iota(t["kcol"][:, :], [[1, T]], base=0, channel_multiplier=0,
                   allow_small_or_imprecise_dtypes=True)
            g.memset(t["ones1"][:, :], 1.0)
            g.memset(t["ones1f"][:, :], 1.0).then_inc(hg, 1)
        p.emit("gpsimd", g_init)
        hs, _ = p.inc("d_thr", 16)
        p.emit("sync", lambda s, hs=hs: s.dma_start(
            out=t["thr1"][:, :], in_=d_thr.ap()).then_inc(hs, 16))
        p.wait("vector", "g_ini")
        hv, _ = p.inc("v_ini", 1)
        p.emit("vector", lambda v, hv=hv: v.tensor_scalar(
            t["ident"][:, :], dmp[:, 0:128], 0.0, None, OP.is_equal).then_inc(hv, 1))
        p.wait("tensor", "v_ini")
        p.wait("tensor", "d_thr")
        ht, _ = p.inc("t_ini", 1)
        p.emit("tensor", lambda te, ht=ht: te.matmul(
            ps_i[:, :], t["ones1f"][:, :], t["thr1"][:, :],
            start=True, stop=True).then_inc(ht, 1))
        p.wait("vector", "t_ini")
        hv2, _ = p.inc("v_msk", 1)

        def v_mask(v, hv2=hv2):
            p.rt_chain(v, v.tensor_copy(t["thrc"][:, :], ps_i[:, :]))
            p.rt_chain(v, v.tensor_scalar(t["mask"][:, 0, :], dmp[:, :], t["thrc"][:, :], NEG,
                                          OP.is_gt, OP.mult))
            p.rt_chain(v, v.tensor_scalar(dmp[:, :], dmp[:, :], 128.0, None, OP.subtract))
            v.tensor_scalar(t["mask"][:, 1, :], dmp[:, :], t["thrc"][:, :], NEG,
                            OP.is_gt, OP.mult).then_inc(hv2, 1)
        p.emit("vector", v_mask)

        hw, _ = p.inc("d_w", 16)
        p.emit("gpsimd", lambda g, hw=hw: g.dma_start(
            out=t["wq"][:, :, :],
            in_=d_wattnT.ap().rearrange("(c p) f -> p c f", p=128)[:, :, 0:D]
        ).then_inc(hw, 16))

        with ExitStack() as actx:
            NCH = 0 if skip_a else nch
            wkv = sb(actx, nc, "wkv", [128, DC, 2 * D], BF16)
            hwk, _ = p.inc("d_w", 16)
            p.emit("gpsimd", lambda g, hwk=hwk: g.dma_start(
                out=wkv[:, :, :],
                in_=d_wattnT.ap().rearrange("(c p) f -> p c f", p=128)[:, :, D:3 * D]
            ).then_inc(hwk, 16))
            xin = [sb(actx, nc, f"xin{i}", [128, 4, 1024], F32) for i in range(2)]
            xnt = [sb(actx, nc, f"xnt{i}", [128, 4, 1024], BF16) for i in range(2)]
            xnf = [sb(actx, nc, f"xnf{i}", [128, DC, 512], BF16) for i in range(2)]
            kbuf = [sb(actx, nc, f"kbuf{i}", [128, DC, 512], BF16) for i in range(2)]
            vbuf = [sb(actx, nc, f"vbuf{i}", [128, DC, 512], BF16) for i in range(2)]
            lns = make_ln_scratch(actx, nc, "lnA")
            ps_x = [ps(actx, nc, f"ps_x{i}", [128, DC, 128], BF16) for i in range(2)]
            ps_k = [ps(actx, nc, f"ps_k{i}", [128, 512], F32) for i in range(2)]
            ps_v = [ps(actx, nc, f"ps_v{i}", [128, 512], F32) for i in range(2)]

            xf_r = d_xf.ap().rearrange("(c p) f -> p c f", p=128)
            ksc_r = d_ksc.ap().rearrange("(c p) t -> p c t", p=128)
            vsc_r = d_vsc.ap().rearrange("(c p) t -> p c t", p=128)

            for ci in range(NCH):
                bi = ci % 2
                if ci >= 2:
                    p.wait("sync", "a_ln", 4 * (ci - 1))
                hx, _ = p.inc(f"a_dx{bi}", 16)
                p.emit("sync", lambda s, ci=ci, bi=bi, hx=hx: s.dma_start(
                    out=xin[bi][:, :, :], in_=xf_r[:, 4 * ci:4 * ci + 4, :]).then_inc(hx, 16))
                if ci >= 2:
                    p.wait("vector", "a_tr", 4 * ci - 4)
                for j in range(4):
                    emit_ln_tile(p, xin[bi][:, j, :], xnt[bi][:, j, :], lns, "a_ln",
                                 wait_before=(f"a_dx{bi}", None) if j == 0 else None)
                for j in range(4):
                    pj = ps_x[j % 2]
                    if ci > 0 or j >= 2:
                        p.wait("tensor", "a_xf", 4 * ci + j - 1)
                    p.wait("tensor", "a_ln", 4 * ci + j + 1)
                    htr, _ = p.inc("a_tr", 1)

                    def t_tr(te, bi=bi, j=j, pj=pj, htr=htr):
                        for k in range(DC):
                            inst = te.transpose(pj[:, k, :], xnt[bi][:, j, 128 * k:128 * (k + 1)],
                                                t["ident"][:, :])
                        inst.then_inc(htr, 1)
                    p.emit("tensor", t_tr)
                    p.wait("vector", "a_tr", 4 * ci + j + 1)
                    if ci >= 2 and j == 0:
                        p.wait("vector", "a_tk", 8 * (ci - 1))
                        p.wait("vector", "a_tv", 8 * (ci - 1))
                    hxf, _ = p.inc("a_xf", 1)
                    p.emit("vector", lambda v, bi=bi, j=j, pj=pj, hxf=hxf: v.tensor_copy(
                        xnf[bi][:, :, 128 * j:128 * (j + 1)], pj[:, :, :]).then_inc(hxf, 1))
                p.wait("tensor", "a_xf", 4 * (ci + 1))
                if ci == 0:
                    p.wait("tensor", "d_w", 32)
                for mc in range(DC):
                    pk = ps_k[mc % 2]
                    if ci > 0 or mc >= 2:
                        p.wait("tensor", "a_ke", 8 * ci + mc - 1)
                    htk, _ = p.inc("a_tk", 1)

                    def t_k(te, bi=bi, mc=mc, pk=pk, htk=htk):
                        for kc in range(DC):
                            inst = te.matmul(pk[:, :], wkv[:, kc, 128 * mc:128 * (mc + 1)],
                                             xnf[bi][:, kc, :], start=(kc == 0), stop=(kc == DC - 1))
                        inst.then_inc(htk, 1)
                    p.emit("tensor", t_k)
                    p.wait("scalar", "a_tk", 8 * ci + mc + 1)
                    if ci >= 2 and mc == 0:
                        p.wait("scalar", f"a_dk{ci % 2}")
                    hke, _ = p.inc("a_ke", 1)
                    p.emit("scalar", lambda s, bi=bi, mc=mc, pk=pk, hke=hke: s.activation(
                        kbuf[bi][:, mc, :], pk[:, :], AF.Copy).then_inc(hke, 1))
                for mc in range(DC):
                    pv = ps_v[mc % 2]
                    if ci > 0 or mc >= 2:
                        p.wait("tensor", "a_ve", 8 * ci + mc - 1)
                    htv, _ = p.inc("a_tv", 1)

                    def t_v(te, bi=bi, mc=mc, pv=pv, htv=htv):
                        for kc in range(DC):
                            inst = te.matmul(pv[:, :], wkv[:, kc, D + 128 * mc:D + 128 * (mc + 1)],
                                             xnf[bi][:, kc, :], start=(kc == 0), stop=(kc == DC - 1))
                        inst.then_inc(htv, 1)
                    p.emit("tensor", t_v)
                    p.wait("vector", "a_tv", 8 * ci + mc + 1)
                    if ci >= 2 and mc == 0:
                        p.wait("vector", f"a_dv{ci % 2}")
                    hve, _ = p.inc("a_ve", 1)
                    p.emit("vector", lambda v, bi=bi, mc=mc, pv=pv, hve=hve: v.tensor_copy(
                        vbuf[bi][:, mc, :], pv[:, :]).then_inc(hve, 1))
                p.wait("sync", "a_ke", 8 * (ci + 1))
                hdk, _ = p.inc(f"a_dk{ci % 2}", 16)
                p.emit("sync", lambda s, bi=bi, ci=ci, hdk=hdk: s.dma_start(
                    out=ksc_r[:, :, 512 * ci:512 * (ci + 1)], in_=kbuf[bi][:, :, :]).then_inc(hdk, 16))
                p.wait("sync", "a_ve", 8 * (ci + 1))
                hdv, _ = p.inc(f"a_dv{ci % 2}", 16)
                p.emit("sync", lambda s, bi=bi, ci=ci, hdv=hdv: s.dma_start(
                    out=vsc_r[:, :, 512 * ci:512 * (ci + 1)], in_=vbuf[bi][:, :, :]).then_inc(hdv, 16))

            if NCH:
                for par in range(2):
                    p.wait("sync", f"a_dk{par}")
                    p.wait("sync", f"a_dv{par}")
            p.barrier()

    if debug_kv:
        with nc.sbuf_tensor("zz", [128, 16, 1024], F32) as zz:
            hz, _ = p.inc("zz", 1)
            p.emit("gpsimd", lambda g: g.memset(zz[:, :, :], 0.0).then_inc(hz, 1))
            p.wait("sync", "zz")
            hd, _ = p.inc("d_x1", 16)
            p.emit("sync", lambda s: s.dma_start(
                out=d_x1.ap().rearrange("(c p) f -> p c f", p=128), in_=zz[:, :, :]).then_inc(hd, 16))
            p.wait("sync", "d_x1")
        return True

    t["xnf"] = sb(ctx, nc, "xnf_own", [128, DC, TOK], BF16, side="right")
    t["yfm"] = sb(era1, nc, "yfm", [128, DC, TOK], BF16)
    xo_r = d_xo.ap().rearrange("(c p) f -> p c f", p=128)
    with ExitStack() as bctx:
        bxin = [sb(bctx, nc, f"b_bxin{i}", [128, 1024], F32) for i in range(2)]
        bxnt = [sb(bctx, nc, f"b_bxnt{i}", [128, 1024], BF16) for i in range(2)]
        lns = make_ln_scratch(bctx, nc, "lnB")
        bps_x = [ps(bctx, nc, f"b_psx{i}", [128, DC, 128], BF16) for i in range(2)]
        for pt in range(16):
            bi = pt % 2
            if pt >= 2:
                p.wait("sync", "b_ln", pt - 1)
            hx, _ = p.inc(f"b_dx{bi}", 16)
            p.emit("sync", lambda s, pt=pt, bi=bi, hx=hx: s.dma_start(
                out=bxin[bi][:, :], in_=xo_r[:, pt, :]).then_inc(hx, 16))
            if pt >= 2:
                p.wait("vector", "b_tr", pt - 1)
            emit_ln_tile(p, bxin[bi][:, :], bxnt[bi][:, :], lns, "b_ln",
                         wait_before=(f"b_dx{bi}", None))
            p.wait("tensor", "b_ln", pt + 1)
            if pt >= 2:
                p.wait("tensor", "b_xf", pt - 1)
            htr, _ = p.inc("b_tr", 1)

            def t_tr(te, bi=bi, htr=htr):
                for k in range(DC):
                    inst = te.transpose(bps_x[bi][:, k, :], bxnt[bi][:, 128 * k:128 * (k + 1)],
                                        t["ident"][:, :])
                inst.then_inc(htr, 1)
            p.emit("tensor", t_tr)
            p.wait("vector", "b_tr", pt + 1)
            hxf, _ = p.inc("b_xf", 1)
            p.emit("vector", lambda v, bi=bi, pt=pt, hxf=hxf: v.tensor_copy(
                t["xnf"][:, :, 128 * pt:128 * (pt + 1)], bps_x[bi][:, :, :]).then_inc(hxf, 1))
        p.barrier()

    if upto == "b0":
        _emit_dummy(p, nc, d_x1)
        return True

    emit_l0_attention(p, ctx, nc, t, d_ksc, d_vsc)
    p.barrier()

    if upto == "attn":
        _emit_dummy(p, nc, d_x1)
        return True

    t["s0"] = sb(ctx, nc, "s0", [128, 16, 1024], BF16, side="right")
    with ExitStack() as cctx:
        wproj = sb(cctx, nc, "wproj", [128, DC, D], BF16)
        hwp, _ = p.inc("c_dw", 16)
        p.emit("gpsimd", lambda g, hwp=hwp: g.dma_start(
            out=wproj[:, :, :],
            in_=d_wprojT.ap().rearrange("(c p) f -> p c f", p=128)).then_inc(hwp, 16))
        p.wait("tensor", "c_dw", 16)
        ps_p = [ps(cctx, nc, f"c_ps{i}", [128, 512], F32) for i in range(2)]
        for mt in range(16):
            for nn in range(2):
                i2 = mt * 2 + nn
                pp = ps_p[i2 % 2]
                if i2 >= 2:
                    p.wait("tensor", "c_pe", i2 - 1)
                htp, _ = p.inc("c_tp", 1)

                def t_p(te, mt=mt, nn=nn, pp=pp, htp=htp):
                    for kc in range(DC):
                        inst = te.matmul(pp[:, :], t["yfm"][:, kc, 128 * mt:128 * (mt + 1)],
                                         wproj[:, kc, 512 * nn:512 * (nn + 1)],
                                         start=(kc == 0), stop=(kc == DC - 1))
                    inst.then_inc(htp, 1)
                p.emit("tensor", t_p)
                p.wait("scalar", "c_tp", i2 + 1)
                hpe, _ = p.inc("c_pe", 1)
                p.emit("scalar", lambda s, mt=mt, nn=nn, pp=pp, hpe=hpe: s.activation(
                    t["s0"][:, mt, 512 * nn:512 * (nn + 1)], pp[:, :], AF.Copy).then_inc(hpe, 1))
        p.barrier()

    if upto == "c":
        era1.close()
        _finish_dummy(p, nc, d_x1)
        return nc

    era1.close()

    t["acc"] = sb(ctx, nc, "acc", [128, 16, 1024], F32, side="right")
    hz, _ = p.inc("d_z", 1)
    p.emit("vector", lambda v, hz=hz: v.memset(t["acc"][:, :, :], 0.0).then_inc(hz, 1))
    emit_ffn_chunked(
        p, ctx, nc, t,
        xn_fm=t["xnf"], acc=t["acc"],
        wg_dram=d_wg0T, wu_dram=d_wu0T, wd_dram=d_wd0T,
        n_ic=6, isz=512, silu_scale=DEQ, wacc_scalar=DEQ * DEQ,
        rw=None, tag="f0", zwait="d_z",
    )
    p.barrier()

    if upto == "d":
        _emit_dummy(p, nc, d_x1)
        return True

    x1_r = d_x1.ap().rearrange("(c p) f -> p c f", p=128)
    with ExitStack() as ectx:
        gbuf = [sb(ectx, nc, f"e_g{i}", [128, 1024], BF16) for i in range(2)]
        xo2 = [sb(ectx, nc, f"e_x{i}", [128, 1024], F32) for i in range(2)]
        tbuf = [sb(ectx, nc, f"e_t{i}", [128, 1024], F32) for i in range(2)]
        for pt in range(16):
            bi = pt % 2
            if pt >= 2:
                p.wait("sync", "e_v", pt - 1)
            hx, _ = p.inc(f"e_dx{bi}", 16)
            p.emit("sync", lambda s, pt=pt, bi=bi, hx=hx: s.dma_start(
                out=xo2[bi][:, :], in_=xo_r[:, pt, :]).then_inc(hx, 16))
            if pt >= 2:
                p.wait("scalar", "e_v", pt - 1)
            hg, _ = p.inc("e_s", 1)
            p.emit("scalar", lambda s, pt=pt, bi=bi, hg=hg: s.activation(
                gbuf[bi][:, :], t["acc"][:, pt, :], AF.Silu).then_inc(hg, 1))
            p.wait("vector", "e_s", pt + 1)
            p.wait("vector", f"e_dx{bi}")
            if pt >= 2:
                p.wait("vector", f"e_dw{pt % 2}")
            hv, _ = p.inc("e_v", 1)

            def v_comb(v, pt=pt, bi=bi, hv=hv):
                p.rt_chain(v, v.tensor_tensor(
                    tbuf[bi][:, :], t["s0"][:, pt, :], gbuf[bi][:, :], OP.mult))
                v.tensor_tensor(tbuf[bi][:, :], xo2[bi][:, :], tbuf[bi][:, :], OP.add).then_inc(hv, 1)
            p.emit("vector", v_comb)
            p.wait("sync", "e_v", pt + 1)
            hw, _ = p.inc(f"e_dw{pt % 2}", 16)
            p.emit("sync", lambda s, pt=pt, bi=bi, hw=hw: s.dma_start(
                out=x1_r[:, pt, :], in_=tbuf[bi][:, :]).then_inc(hw, 16))
        p.wait("sync", "e_dw0")
        p.wait("sync", "e_dw1")
    return False


def emit_l0_attention(p, ctx, nc, t, d_ksc, d_vsc):
    with ExitStack() as actx:
        kh = sb(actx, nc, "kh", [128, NTOK // 2], BF16)
        vf = sb(actx, nc, "vf", [128, NTOK // 2], BF16)
        vt = sb(actx, nc, "vt", [128, 64, 128], BF16)
        qh = sb(actx, nc, "qh", [128, TOK], BF16)
        bias = sb(actx, nc, "bias", [128, 2, T], F32)
        bb = [sb(actx, nc, f"bb{i}", [128, T], F32) for i in range(2)]
        bsc = [sb(actx, nc, f"bsc{i}", [128, 512], F32) for i in range(2)]
        pb = [sb(actx, nc, f"pb{i}", [128, T], BF16) for i in range(2)]
        at_sb = sb(actx, nc, "at_sb", [128, 16, 128], BF16)
        rmax = [sb(actx, nc, f"rmax{i}", [128, 1], F32) for i in range(2)]
        rsum = [sb(actx, nc, f"rsum{i}", [128, 1], F32) for i in range(2)]
        rrec = [sb(actx, nc, f"rrec{i}", [128, 1], F32) for i in range(2)]
        ps_q = [ps(actx, nc, f"a_psq{i}", [128, 512], F32) for i in range(2)]
        ps_sc = [ps(actx, nc, f"a_pssc{i}", [128, 512], F32) for i in range(2)]
        ps_at = [ps(actx, nc, f"a_psat{i}", [128, 8, 128], BF16) for i in range(2)]
        ps_av = ps(actx, nc, "a_psav", [128, 128], F32)
        ps_vt = ps(actx, nc, "a_psvt", [128, DC, 128], BF16)

        ksc_r = d_ksc.ap().rearrange("(c p) t -> p c t", p=128)
        vsc_r = d_vsc.ap().rearrange("(c p) t -> p c t", p=128)

        n_it = 0
        n_sc = 0
        n_tt = 0
        n_vt = 0
        pending = [None]

        def flush():
            if pending[0] is not None:
                p.emit_deferred(pending[0])
                pending[0] = None
        for hc in range(DC):
            if hc > 0:
                p.wait("scalar", "h_ts", n_sc)
            for nc4 in range(4):
                pq = ps_q[nc4 % 2]
                if hc > 0 or nc4 >= 2:
                    p.wait("tensor", "h_qe", 4 * hc + nc4 - 1)
                htq, _ = p.inc("h_tq", 1)

                def t_q(te, hc=hc, nc4=nc4, pq=pq, htq=htq):
                    for kc in range(DC):
                        inst = te.matmul(pq[:, :], t["wq"][:, kc, 128 * hc:128 * (hc + 1)],
                                         t["xnf"][:, kc, 512 * nc4:512 * (nc4 + 1)],
                                         start=(kc == 0), stop=(kc == DC - 1))
                    inst.then_inc(htq, 1)
                p.emit("tensor", t_q)
                p.wait("scalar", "h_tq", 4 * hc + nc4 + 1)
                hqe, _ = p.inc("h_qe", 1)
                p.emit("scalar", lambda s, nc4=nc4, pq=pq, hqe=hqe: s.activation(
                    qh[:, 512 * nc4:512 * (nc4 + 1)], pq[:, :], AF.Copy).then_inc(hqe, 1))
            for half in range(2):
                hidx = 2 * hc + half
                flush()
                if hidx >= 1:
                    p.wait("sync", "h_ts", 64 * hidx)
                    p.wait("sync", "h_tv", 8 * hidx)
                hk, _ = p.inc("h_dk", 32)
                p.emit("sync", lambda s, hc=hc, half=half, hk=hk: (
                    s.dma_start(out=kh[:, :], in_=ksc_r[:, hc, 8192 * half:8192 * (half + 1)]
                                ).then_inc(hk, 16),
                    s.dma_start(out=vf[:, :], in_=vsc_r[:, hc, 8192 * half:8192 * (half + 1)]
                                ).then_inc(hk, 16)))
                p.wait("tensor", "h_dk", 32 * (hidx + 1))
                if hidx >= 1:
                    p.wait("vector", "h_tav", n_it)
                for g8 in range(8):
                    n_vt += 1
                    if n_vt >= 2:
                        p.wait("tensor", "h_ve", n_vt - 1)
                    hvt, _ = p.inc("h_tv", 1)

                    def t_vt(te, g8=g8, hvt=hvt):
                        for k in range(8):
                            kt = 8 * g8 + k
                            inst = te.transpose(ps_vt[:, k, :], vf[:, 128 * kt:128 * (kt + 1)],
                                                t["ident"][:, :])
                        inst.then_inc(hvt, 1)
                    p.emit("tensor", t_vt)
                    p.wait("vector", "h_tv", n_vt)
                    hve, _ = p.inc("h_ve", 1)
                    p.emit("vector", lambda v, g8=g8, hve=hve: v.tensor_copy(
                        vt[:, 8 * g8:8 * (g8 + 1), :], ps_vt[:, :, :]).then_inc(hve, 1))
                for h2 in range(2):
                    h = 2 * hc + h2
                    slope = ALIBI[h]
                    p.wait("vector", "v_msk", 1)
                    if n_sc > 0:
                        p.wait("vector", "h_gb", n_sc)
                    hbi, _ = p.inc("h_bias", 1)
                    nbias = p.rel("h_bias")

                    def v_bias(v, slope=slope, hbi=hbi):
                        v.scalar_tensor_tensor(bias[:, 0, :], t["kcol"][:, :], slope,
                                               t["mask"][:, 0, :], OP.mult, OP.add)
                        v.scalar_tensor_tensor(bias[:, 1, :], t["kcol"][:, :], slope,
                                               t["mask"][:, 1, :], OP.mult, OP.add).then_inc(hbi, 1)
                    p.emit("vector", v_bias)
                    for r4 in range(4):
                        r = 4 * half + r4
                        for qt in range(2):
                            it = n_it
                            n_it += 1
                            bbuf = bb[it % 2]
                            pbuf = pb[it % 2]
                            q0 = 256 * r + 128 * qt
                            k0 = 2048 * r4
                            if it == 0 or (half == 0 and h2 == 0 and r4 == 0 and qt == 0):
                                p.wait("tensor", "h_qe", 4 * hc + 4)
                            p.wait("gpsimd", "h_bias", nbias)
                            if it >= 2:
                                p.wait("gpsimd", "h_se", it - 1)
                            for ksb in range(4):
                                n_sc += 1
                                psc = ps_sc[n_sc % 2]
                                if n_sc >= 3:
                                    p.wait("tensor", "h_sc2", n_sc - 2)
                                hts, _ = p.inc("h_ts", 1)

                                def t_s(te, h2=h2, q0=q0, k0=k0, ksb=ksb, psc=psc, hts=hts):
                                    te.matmul(psc[:, :],
                                              qh[64 * h2:64 * (h2 + 1), q0:q0 + 128],
                                              kh[64 * h2:64 * (h2 + 1), k0 + 512 * ksb:k0 + 512 * (ksb + 1)],
                                              start=True, stop=True).then_inc(hts, 1)
                                p.emit("tensor", t_s)
                                p.wait("scalar", "h_ts", n_sc)
                                if n_sc >= 3:
                                    p.wait("scalar", "h_gb", n_sc - 2)
                                hsc2, _ = p.inc("h_sc2", 1)
                                p.emit("scalar", lambda s, psc=psc, n_sc=n_sc, hsc2=hsc2:
                                       s.activation(bsc[n_sc % 2][:, :], psc[:, :], AF.Copy,
                                                    scale=QSCALE).then_inc(hsc2, 1))
                                p.wait("gpsimd", "h_sc2", n_sc)
                                hvb, _ = p.inc("h_gb", 1)

                                def g_b(g, qt=qt, ksb=ksb, n_sc=n_sc, hvb=hvb, bbuf=bbuf):
                                    g.tensor_tensor(
                                        bbuf[:, 512 * ksb:512 * (ksb + 1)], bsc[n_sc % 2][:, :],
                                        bias[:, qt, 512 * ksb:512 * (ksb + 1)], OP.add
                                    ).then_inc(hvb, 1)
                                p.emit("gpsimd", g_b)
                            p.wait("vector", "h_gb", n_sc)
                            if it >= 2:
                                p.wait("vector", "h_se", it - 1)
                            hvm, _ = p.inc("h_vm", 1)
                            p.emit("vector", lambda v, bbuf=bbuf, it=it, hvm=hvm: v.tensor_reduce(
                                rmax[it % 2][:, :], bbuf[:, :], AX.X, OP.max,
                                negate=True).then_inc(hvm, 1))
                            p.wait("scalar", "h_vm", n_it)
                            if it >= 2:
                                p.wait("scalar", "h_tt", 2 * (it - 1))
                            if it >= 2:
                                p.wait("scalar", "h_vr", it - 1)
                            hse, _ = p.inc("h_se", 1)
                            p.emit("scalar", lambda s, bbuf=bbuf, pbuf=pbuf, it=it, hse=hse:
                                   s.activation(pbuf[:, :], bbuf[:, :], AF.Exp,
                                                bias=rmax[it % 2][:, :],
                                                accum_out=rsum[it % 2][:, :]).then_inc(hse, 1))
                            prev = pending[0]
                            p.defer_start()
                            p.wait("vector", "h_se", n_it)
                            hvr, _ = p.inc("h_vr", 1)

                            def v_norm(v, pbuf=pbuf, it=it, hvr=hvr):
                                p.rt_chain(v, v.reciprocal(rrec[it % 2][:, :], rsum[it % 2][:, :]))
                                v.tensor_scalar(pbuf[:, :], pbuf[:, :], rrec[it % 2][:, :], None,
                                                OP.mult).then_inc(hvr, 1)
                            p.emit("vector", v_norm)
                            p.wait("tensor", "h_vr", n_it)
                            for g in range(2):
                                n_tt += 1
                                if n_tt >= 3:
                                    p.wait("tensor", "h_ae", n_tt - 2)
                                htt, _ = p.inc("h_tt", 1)

                                def t_at(te, pbuf=pbuf, g=g, htt=htt):
                                    pa = ps_at[g]
                                    for k in range(8):
                                        kt = 8 * g + k
                                        inst = te.transpose(pa[:, k, :],
                                                            pbuf[:, 128 * kt:128 * (kt + 1)],
                                                            t["ident"][:, :])
                                    inst.then_inc(htt, 1)
                                p.emit("tensor", t_at)
                                p.wait("vector", "h_tt", n_tt)
                                if n_tt >= 3:
                                    p.wait("vector", "h_tav", it)
                                hae, _ = p.inc("h_ae", 1)
                                p.emit("vector", lambda v, g=g, hae=hae: v.tensor_copy(
                                    at_sb[:, 8 * g:8 * (g + 1), :], ps_at[g][:, :, :]).then_inc(hae, 1))
                            p.wait("tensor", "h_ae", n_tt)
                            if it >= 1:
                                p.wait("tensor", "h_vy", it)
                            hav, _ = p.inc("h_tav", 1)

                            def t_av(te, h2=h2, r4=r4, hav=hav):
                                for kt in range(16):
                                    inst = te.matmul(ps_av[0:64, :],
                                                     vt[:, 16 * r4 + kt, 64 * h2:64 * (h2 + 1)],
                                                     at_sb[:, kt, :],
                                                     start=(kt == 0), stop=(kt == 15))
                                inst.then_inc(hav, 1)
                            p.emit("tensor", t_av)
                            p.wait("vector", "h_tav", n_it)
                            hvy, _ = p.inc("h_vy", 1)
                            p.emit("vector", lambda v, h2=h2, hc=hc, q0=q0, hvy=hvy: v.tensor_copy(
                                t["yfm"][64 * h2:64 * (h2 + 1), hc, q0:q0 + 128],
                                ps_av[0:64, :]).then_inc(hvy, 1))
                            pending[0] = p.defer_end()
                            if prev:
                                p.emit_deferred(prev)
        flush()


def emit_ffn_chunked(p, ctx, nc, t, xn_fm, acc, wg_dram, wu_dram, wd_dram,
                     n_ic, isz, silu_scale, wacc_scalar, rw, tag, zwait=None):
    nsub = isz // 128
    with ExitStack() as fctx:
        wg_b = [sb(fctx, nc, f"{tag}_wg{i}", [128, DC, isz], BF16) for i in range(2)]
        wu_b = [sb(fctx, nc, f"{tag}_wu{i}", [128, DC, isz], BF16) for i in range(2)]
        wd_b = [sb(fctx, nc, f"{tag}_wd{i}", [128, nsub, 1024], BF16) for i in range(2)]
        hb = sb(fctx, nc, f"{tag}_h", [128, nsub, TOK], BF16)
        sg = [sb(fctx, nc, f"{tag}_sg{i}", [128, 512], BF16) for i in range(2)]
        pg = [ps(fctx, nc, f"{tag}_pg{i}", [128, 512], F32) for i in range(2)]
        pu = [ps(fctx, nc, f"{tag}_pu{i}", [128, 512], F32) for i in range(2)]
        pd = [ps(fctx, nc, f"{tag}_pd{i}", [128, 512], F32) for i in range(2)]

        NT = TOK // 512
        ig = 0
        jd = 0
        for ic in range(n_ic):
            bi = ic % 2
            if ic >= 2:
                p.wait("gpsimd", f"{tag}_tg", nsub * NT * (ic - 1))
                p.wait("gpsimd", f"{tag}_tu", nsub * NT * (ic - 1))
                p.wait("gpsimd", f"{tag}_td", 2 * 16 * (ic - 1))
            hw, _ = p.inc(f"{tag}_dw{bi}", 48)

            def g_w(g, ic=ic, bi=bi, hw=hw):
                g.dma_start(out=wg_b[bi][:, :, :],
                            in_=wg_dram.ap()[:, isz * ic:isz * (ic + 1)]
                            .rearrange("(c p) f -> p c f", p=128)).then_inc(hw, 16)
                g.dma_start(out=wu_b[bi][:, :, :],
                            in_=wu_dram.ap()[:, isz * ic:isz * (ic + 1)]
                            .rearrange("(c p) f -> p c f", p=128)).then_inc(hw, 16)
                g.dma_start(out=wd_b[bi][:, :, :],
                            in_=wd_dram.ap()[isz * ic:isz * (ic + 1), :]
                            .rearrange("(c p) f -> p c f", p=128)).then_inc(hw, 16)
            p.emit("gpsimd", g_w)
            p.wait("tensor", f"{tag}_dw{bi}")
            if ic >= 1:
                p.wait("vector", f"{tag}_td", 2 * 16 * ic, )
            for mc in range(nsub):
                for nc4 in range(NT):
                    ig += 1
                    pgi = pg[ig % 2]
                    pui = pu[ig % 2]
                    if ig >= 3:
                        p.wait("tensor", f"{tag}_sg", ig - 2)
                    htg, _ = p.inc(f"{tag}_tg", 1)

                    def t_g(te, bi=bi, mc=mc, nc4=nc4, pgi=pgi, htg=htg):
                        for kc in range(DC):
                            inst = te.matmul(pgi[:, :], wg_b[bi][:, kc, 128 * mc:128 * (mc + 1)],
                                             xn_fm[:, kc, 512 * nc4:512 * (nc4 + 1)],
                                             start=(kc == 0), stop=(kc == DC - 1))
                        inst.then_inc(htg, 1)
                    p.emit("tensor", t_g)
                    if ig >= 3:
                        p.wait("tensor", f"{tag}_h", ig - 2)
                    htu, _ = p.inc(f"{tag}_tu", 1)

                    def t_u(te, bi=bi, mc=mc, nc4=nc4, pui=pui, htu=htu):
                        for kc in range(DC):
                            inst = te.matmul(pui[:, :], wu_b[bi][:, kc, 128 * mc:128 * (mc + 1)],
                                             xn_fm[:, kc, 512 * nc4:512 * (nc4 + 1)],
                                             start=(kc == 0), stop=(kc == DC - 1))
                        inst.then_inc(htu, 1)
                    p.emit("tensor", t_u)
                    p.wait("scalar", f"{tag}_tg", ig)
                    if ig >= 3:
                        p.wait("scalar", f"{tag}_h", ig - 2)
                    hsg, _ = p.inc(f"{tag}_sg", 1)
                    p.emit("scalar", lambda s, pgi=pgi, ig=ig, hsg=hsg: s.activation(
                        sg[ig % 2][:, :], pgi[:, :], AF.Silu,
                        scale=silu_scale).then_inc(hsg, 1))
                    p.wait("vector", f"{tag}_tu", ig)
                    p.wait("vector", f"{tag}_sg", ig)
                    hh, _ = p.inc(f"{tag}_h", 1)
                    p.emit("vector", lambda v, bi=bi, mc=mc, nc4=nc4, pui=pui, ig=ig, hh=hh:
                           v.tensor_tensor(hb[:, mc, 512 * nc4:512 * (nc4 + 1)],
                                           pui[:, :], sg[ig % 2][:, :], OP.mult).then_inc(hh, 1))
            p.wait("tensor", f"{tag}_h", ig)
            for mt in range(16):
                for nn in range(2):
                    jd += 1
                    pdi = pd[jd % 2]
                    if jd >= 3:
                        p.wait("tensor", f"{tag}_wa", jd - 2)
                    htd, _ = p.inc(f"{tag}_td", 1)

                    def t_d(te, bi=bi, mt=mt, nn=nn, pdi=pdi, htd=htd):
                        for kc in range(nsub):
                            inst = te.matmul(pdi[:, :], hb[:, kc, 128 * mt:128 * (mt + 1)],
                                             wd_b[bi][:, kc, 512 * nn:512 * (nn + 1)],
                                             start=(kc == 0), stop=(kc == nsub - 1))
                        inst.then_inc(htd, 1)
                    p.emit("tensor", t_d)
                    p.wait("vector", f"{tag}_td", jd)
                    if zwait and jd == 1:
                        p.wait("vector", zwait, 1)
                    hwa, _ = p.inc(f"{tag}_wa", 1)
                    scal = wacc_scalar if rw is None else rw(mt)
                    p.emit("vector", lambda v, mt=mt, nn=nn, pdi=pdi, scal=scal, hwa=hwa:
                           v.scalar_tensor_tensor(acc[:, mt, 512 * nn:512 * (nn + 1)],
                                                  pdi[:, :], scal,
                                                  acc[:, mt, 512 * nn:512 * (nn + 1)],
                                                  OP.mult, OP.add).then_inc(hwa, 1))
        p.wait("vector", f"{tag}_wa", jd)




def emit_routing(p, ctx, nc, t, xn_fm, gate_wT_sb, bias_bc, rw):
    with ExitStack() as rctx:
        ps_g = [ps(rctx, nc, f"r_ps{i}", [128, 64], F32) for i in range(2)]
        sco = [sb(rctx, nc, f"r_sco{i}", [128, 64], F32) for i in range(2)]
        sfc = sb(rctx, nc, "r_sfc", [128, 64], F32)
        grp2 = sb(rctx, nc, "r_grp2", [128, 64], F32)
        m1 = sb(rctx, nc, "r_m1", [128, 8], F32)
        m2 = sb(rctx, nc, "r_m2", [128, 8], F32)
        gs = sb(rctx, nc, "r_gs", [128, 8], F32)
        gm = sb(rctx, nc, "r_gm", [128, 1], F32)
        gsel = sb(rctx, nc, "r_gsel", [128, 8], F32)
        gmask = sb(rctx, nc, "r_gmask", [128, 8], F32)
        wm = sb(rctx, nc, "r_wm", [128, 64], F32)
        msk = sb(rctx, nc, "r_msk", [128, 64], F32)
        mm = sb(rctx, nc, "r_mm", [128, 1], F32)
        sel = sb(rctx, nc, "r_sel", [128, 64], F32)
        den = sb(rctx, nc, "r_den", [128, 1], F32)
        for pt in range(16):
            if pt >= 2:
                p.wait("tensor", "r_sig", pt - 1)
            htr, _ = p.inc("r_tg", 1)

            def t_r(te, pt=pt, htr=htr):
                for kc in range(DC):
                    inst = te.matmul(ps_g[pt % 2][:, :], xn_fm[:, kc, 128 * pt:128 * (pt + 1)],
                                     gate_wT_sb[:, kc, :], start=(kc == 0), stop=(kc == DC - 1))
                inst.then_inc(htr, 1)
            p.emit("tensor", t_r)
            p.wait("scalar", "r_tg", pt + 1)
            if pt >= 2:
                p.wait("scalar", "r_rw", pt - 1)
            hsg, _ = p.inc("r_sig", 1)
            p.emit("scalar", lambda s, pt=pt, hsg=hsg: s.activation(
                sco[pt % 2][:, :], ps_g[pt % 2][:, :], AF.Sigmoid).then_inc(hsg, 1))
            p.wait("vector", "r_sig", pt + 1)
            hrw, _ = p.inc("r_rw", 1)

            def v_route(v, pt=pt, hrw=hrw):
                C = lambda inst: p.rt_chain(v, inst)
                g3 = sfc[:, :].rearrange("p (g i) -> p g i", i=8)
                C(v.tensor_tensor(sfc[:, :], sco[pt % 2][:, :], bias_bc[:, :], OP.add))
                C(v.tensor_reduce(m1[:, :], g3, AX.X, OP.max))
                C(v.tensor_tensor(grp2[:, :].rearrange("p (g i) -> p g i", i=8), g3,
                                  m1[:, :].unsqueeze(2).broadcast_to((128, 8, 8)), OP.is_equal))
                C(v.scalar_tensor_tensor(grp2[:, :], grp2[:, :], NEG, sfc[:, :], OP.mult, OP.add))
                C(v.tensor_reduce(m2[:, :], grp2[:, :].rearrange("p (g i) -> p g i", i=8), AX.X, OP.max))
                C(v.tensor_tensor(gs[:, :], m1[:, :], m2[:, :], OP.add))
                C(v.memset(gmask[:, :], 0.0))
                for it in range(4):
                    C(v.tensor_reduce(gm[:, :], gs[:, :], AX.X, OP.max))
                    C(v.tensor_scalar(gsel[:, :], gs[:, :], gm[:, :], None, OP.is_equal))
                    C(v.tensor_tensor(gmask[:, :], gmask[:, :], gsel[:, :], OP.add))
                    C(v.scalar_tensor_tensor(gs[:, :], gsel[:, :], NEG, gs[:, :], OP.mult, OP.add))
                C(v.tensor_tensor(msk[:, :].rearrange("p (g i) -> p g i", i=8),
                                  sfc[:, :].rearrange("p (g i) -> p g i", i=8),
                                  gmask[:, :].unsqueeze(2).broadcast_to((128, 8, 8)), OP.mult))
                C(v.memset(wm[:, :], 0.0))
                for it in range(8):
                    C(v.tensor_reduce(mm[:, :], msk[:, :], AX.X, OP.max))
                    C(v.tensor_scalar(sel[:, :], msk[:, :], mm[:, :], None, OP.is_equal))
                    C(v.tensor_tensor(wm[:, :], wm[:, :], sel[:, :], OP.add))
                    C(v.scalar_tensor_tensor(msk[:, :], sel[:, :], NEG, msk[:, :], OP.mult, OP.add))
                C(v.tensor_tensor(wm[:, :], wm[:, :], sco[pt % 2][:, :], OP.mult))
                C(v.tensor_reduce(den[:, :], wm[:, :], AX.X, OP.add))
                C(v.tensor_scalar(den[:, :], den[:, :], 1e-20, None, OP.add))
                C(v.reciprocal(den[:, :], den[:, :]))
                C(v.tensor_scalar(den[:, :], den[:, :], 2.5, None, OP.mult))
                v.tensor_scalar(rw[:, pt, :], wm[:, :], den[:, :], None, OP.mult).then_inc(hrw, 1)
            p.emit("vector", v_route)
        p.wait("vector", "r_rw", 16)


def emit_moe(p, ctx, nc, xn_fm, acc, experts, tag, zwait=None):
    with ExitStack() as fctx:
        imax = max(sum(e["isub"]) for e in experts)
        nsmax = max(len(e["isub"]) for e in experts)
        wg_b = [sb(fctx, nc, f"{tag}_wg{i}", [128, DC, imax], BF16) for i in range(2)]
        wu_b = [sb(fctx, nc, f"{tag}_wu{i}", [128, DC, imax], BF16) for i in range(2)]
        wd_b = [sb(fctx, nc, f"{tag}_wd{i}", [128, nsmax, 1024], BF16) for i in range(2)]
        hb = sb(fctx, nc, f"{tag}_h", [128, nsmax, TOK], BF16)
        sg = [sb(fctx, nc, f"{tag}_sg{i}", [128, 512], BF16) for i in range(2)]
        pg = [ps(fctx, nc, f"{tag}_pg{i}", [128, 512], F32) for i in range(2)]
        pu = [ps(fctx, nc, f"{tag}_pu{i}", [128, 512], F32) for i in range(2)]
        pd = [ps(fctx, nc, f"{tag}_pd{i}", [128, 512], F32) for i in range(2)]
        NT = TOK // 512
        ig = 0
        jd = 0
        n_tgu = [0, 0]
        n_td = [0, 0]
        for ic, ex in enumerate(experts):
            bi = ic % 2
            isub = ex["isub"]
            nsub = len(isub)
            ioffs = [sum(isub[:k]) for k in range(nsub)]
            I = sum(isub)
            if ic >= 2:
                p.wait("gpsimd", f"{tag}_tg", n_tgu[bi])
                p.wait("gpsimd", f"{tag}_tu", n_tgu[bi])
                p.wait("gpsimd", f"{tag}_td", n_td[bi])
            ndma = 48 if ex.get("wd_tail") is None else 64
            hw, _ = p.inc(f"{tag}_dw{bi}", ndma)

            def g_w(g, ex=ex, bi=bi, hw=hw, I=I, nsub=nsub):
                g.dma_start(out=wg_b[bi][:, :, 0:I],
                            in_=ex["wg"].rearrange("(c p) f -> p c f", p=128)).then_inc(hw, 16)
                g.dma_start(out=wu_b[bi][:, :, 0:I],
                            in_=ex["wu"].rearrange("(c p) f -> p c f", p=128)).then_inc(hw, 16)
                nmain = ex["wd_main"].shape[0] // 128
                g.dma_start(out=wd_b[bi][:, 0:nmain, :],
                            in_=ex["wd_main"].rearrange("(c p) f -> p c f", p=128)).then_inc(hw, 16)
                if ex.get("wd_tail") is not None:
                    g.dma_start(out=wd_b[bi][0:ex["wd_tail"].shape[0], nsub - 1, :],
                                in_=ex["wd_tail"]).then_inc(hw, 16)
            p.emit("gpsimd", g_w)
            p.wait("tensor", f"{tag}_dw{bi}")
            if ic >= 1:
                p.wait("vector", f"{tag}_td", jd)
            n_tgu[bi] = 0
            for mc in range(nsub):
                sz = isub[mc]
                io = ioffs[mc]
                for nc4 in range(NT):
                    ig += 1
                    pgi = pg[ig % 2]
                    pui = pu[ig % 2]
                    if ig >= 3:
                        p.wait("tensor", f"{tag}_sg", ig - 2)
                    htg, _ = p.inc(f"{tag}_tg", 1)

                    def t_g(te, bi=bi, io=io, sz=sz, nc4=nc4, pgi=pgi, htg=htg):
                        for kc in range(DC):
                            inst = te.matmul(pgi[0:sz, :], wg_b[bi][:, kc, io:io + sz],
                                             xn_fm[:, kc, 512 * nc4:512 * (nc4 + 1)],
                                             start=(kc == 0), stop=(kc == DC - 1))
                        inst.then_inc(htg, 1)
                    p.emit("tensor", t_g)
                    if ig >= 3:
                        p.wait("tensor", f"{tag}_h", ig - 2)
                    htu, _ = p.inc(f"{tag}_tu", 1)

                    def t_u(te, bi=bi, io=io, sz=sz, nc4=nc4, pui=pui, htu=htu):
                        for kc in range(DC):
                            inst = te.matmul(pui[0:sz, :], wu_b[bi][:, kc, io:io + sz],
                                             xn_fm[:, kc, 512 * nc4:512 * (nc4 + 1)],
                                             start=(kc == 0), stop=(kc == DC - 1))
                        inst.then_inc(htu, 1)
                    p.emit("tensor", t_u)
                    p.wait("scalar", f"{tag}_tg", ig)
                    if ig >= 3:
                        p.wait("scalar", f"{tag}_h", ig - 2)
                    hsg, _ = p.inc(f"{tag}_sg", 1)
                    p.emit("scalar", lambda s, pgi=pgi, sz=sz, ig=ig, hsg=hsg, sc=ex["scale"]:
                           s.activation(sg[ig % 2][0:sz, :], pgi[0:sz, :], AF.Silu,
                                        scale=sc).then_inc(hsg, 1))
                    p.wait("vector", f"{tag}_tu", ig)
                    p.wait("vector", f"{tag}_sg", ig)
                    hh, _ = p.inc(f"{tag}_h", 1)
                    p.emit("vector", lambda v, mc=mc, sz=sz, nc4=nc4, pui=pui, ig=ig, hh=hh:
                           v.tensor_tensor(hb[0:sz, mc, 512 * nc4:512 * (nc4 + 1)],
                                           pui[0:sz, :], sg[ig % 2][0:sz, :], OP.mult).then_inc(hh, 1))
                    n_tgu[bi] += 0
            n_tgu[bi] = p.rel(f"{tag}_tg")
            p.wait("tensor", f"{tag}_h", ig)
            for mt in range(16):
                for nn in range(2):
                    jd += 1
                    pdi = pd[jd % 2]
                    if jd >= 3:
                        p.wait("tensor", f"{tag}_wa", jd - 2)
                    htd, _ = p.inc(f"{tag}_td", 1)

                    def t_d(te, bi=bi, mt=mt, nn=nn, pdi=pdi, htd=htd, isub=isub, ioffs=ioffs):
                        for kc in range(len(isub)):
                            inst = te.matmul(pdi[:, :], hb[0:isub[kc], kc, 128 * mt:128 * (mt + 1)],
                                             wd_b[bi][0:isub[kc], kc, 512 * nn:512 * (nn + 1)],
                                             start=(kc == 0), stop=(kc == len(isub) - 1))
                        inst.then_inc(htd, 1)
                    p.emit("tensor", t_d)
                    p.wait("vector", f"{tag}_td", jd)
                    if zwait and jd == 1:
                        p.wait("vector", zwait, 1)
                    hwa, _ = p.inc(f"{tag}_wa", 1)
                    scal = ex["wscal"] if ex.get("rw") is None else ex["rw"](mt)
                    p.emit("vector", lambda v, mt=mt, nn=nn, pdi=pdi, scal=scal, hwa=hwa:
                           v.scalar_tensor_tensor(acc[:, mt, 512 * nn:512 * (nn + 1)],
                                                  pdi[:, :], scal,
                                                  acc[:, mt, 512 * nn:512 * (nn + 1)],
                                                  OP.mult, OP.add).then_inc(hwa, 1))
            n_td[bi] = p.rel(f"{tag}_td")
        p.wait("vector", f"{tag}_wa", jd)


def _l1_drams(nc, x1_kind="ExternalInput"):
    dd = {}
    if x1_kind is not None:
        dd["x1"] = nc.dram_tensor("x1_own", [TOK, D], F32, kind=x1_kind)
    dd["wq"] = nc.dram_tensor("wqT", [D, D], F32, kind="ExternalInput")
    dd["wk"] = nc.dram_tensor("wkT", [D, D], F32, kind="ExternalInput")
    dd["wv"] = nc.dram_tensor("wvT", [D, D], F32, kind="ExternalInput")
    dd["wo"] = nc.dram_tensor("woT", [D, D], F32, kind="ExternalInput")
    dd["bo"] = nc.dram_tensor("b_o", [1, D], F32, kind="ExternalInput")
    dd["rk"] = nc.dram_tensor("rk_t", [B, B, D], F32, kind="ExternalInput")
    dd["rv"] = nc.dram_tensor("rv_t", [B, B, D], F32, kind="ExternalInput")
    dd["gw"] = nc.dram_tensor("gate_wT", [D, 64], F32, kind="ExternalInput")
    dd["gb"] = nc.dram_tensor("gate_bias", [1, 64], F32, kind="ExternalInput")
    dd["Wg"] = nc.dram_tensor("WgT", [MOE_E, D, MOE_I], F32, kind="ExternalInput")
    dd["Wu"] = nc.dram_tensor("WuT", [MOE_E, D, MOE_I], F32, kind="ExternalInput")
    dd["Wd"] = nc.dram_tensor("WdT", [MOE_E, MOE_I, D], F32, kind="ExternalInput")
    dd["Sg"] = nc.dram_tensor("SgT", [D, S_I], F32, kind="ExternalInput")
    dd["Su"] = nc.dram_tensor("SuT", [D, S_I], F32, kind="ExternalInput")
    dd["Sd"] = nc.dram_tensor("SdT", [S_I, D], F32, kind="ExternalInput")
    dd["y"] = nc.dram_tensor("y_own", [TOK, D], F32, kind="ExternalOutput")
    dd["s1"] = nc.dram_tensor("s1_park", [TOK, D], BF16, kind="Internal")
    return dd


def build_l1(upto="full", n_experts=MOE_E):
    nc = bass.Bass("TRN2", target_bir_lowering=False)
    ctx = ExitStack()
    nc._ctx_keepalive = ctx
    dd = _l1_drams(nc)
    p = Prog(nc, ctx)
    t = {}
    t["ident"] = sb(ctx, nc, "ident", [128, 128], BF16)
    t["ones1"] = sb(ctx, nc, "ones1", [1, 128], BF16)
    t["ones1f"] = sb(ctx, nc, "ones1f", [1, 128], F32)
    with nc.sbuf_tensor("dmp1", [128, 128], F32) as dmp1:
        hg, _ = p.inc("g_ini", 1)

        def g_init(g, hg=hg):
            g.iota(dmp1[:, :], [[1, 128]], base=0, channel_multiplier=-1,
                   allow_small_or_imprecise_dtypes=True)
            g.memset(t["ones1"][:, :], 1.0)
            g.memset(t["ones1f"][:, :], 1.0).then_inc(hg, 1)
        p.emit("gpsimd", g_init)
        p.wait("vector", "g_ini")
        hv, _ = p.inc("v_ini", 1)
        p.emit("vector", lambda v, hv=hv: v.tensor_scalar(
            t["ident"][:, :], dmp1[:, :], 0.0, None, OP.is_equal).then_inc(hv, 1))
        emit_l1_body(p, ctx, nc, t, dd, upto, n_experts)
    with nc.Block() as block:
        p.build(block)
    return nc


def emit_l1_body(p, ctx, nc, t, dd, upto="full", n_experts=MOE_E):
    d_x1 = dd["x1"]
    d_wq, d_wk, d_wv, d_wo, d_bo = dd["wq"], dd["wk"], dd["wv"], dd["wo"], dd["bo"]
    d_rk, d_rv, d_gw, d_gb = dd["rk"], dd["rv"], dd["gw"], dd["gb"]
    d_Wg, d_Wu, d_Wd = dd["Wg"], dd["Wu"], dd["Wd"]
    d_Sg, d_Su, d_Sd = dd["Sg"], dd["Su"], dd["Sd"]
    d_y, d_s1 = dd["y"], dd["s1"]
    t["xnf1"] = sb(ctx, nc, "xn1f", [128, DC, TOK], BF16, side="right")
    x1_r = d_x1.ap().rearrange("(c p) f -> p c f", p=128)

    if True:
        with ExitStack() as bctx:
            bxin = [sb(bctx, nc, f"l1xin{i}", [128, 1024], F32) for i in range(2)]
            bxnt = [sb(bctx, nc, f"l1xnt{i}", [128, 1024], BF16) for i in range(2)]
            lns = make_ln_scratch(bctx, nc, "lnL1")
            bps_x = [ps(bctx, nc, f"l1psx{i}", [128, DC, 128], BF16) for i in range(2)]
            for pt in range(16):
                bi = pt % 2
                if pt >= 2:
                    p.wait("sync", "l_ln", pt - 1)
                hx, _ = p.inc(f"l_dx{bi}", 16)
                p.emit("sync", lambda s, pt=pt, bi=bi, hx=hx: s.dma_start(
                    out=bxin[bi][:, :], in_=x1_r[:, pt, :]).then_inc(hx, 16))
                if pt >= 2:
                    p.wait("vector", "l_tr", pt - 1)
                emit_ln_tile(p, bxin[bi][:, :], bxnt[bi][:, :], lns, "l_ln",
                             wait_before=(f"l_dx{bi}", None))
                p.wait("tensor", "l_ln", pt + 1)
                p.wait("tensor", "v_ini", 1)
                if pt >= 2:
                    p.wait("tensor", "l_xf", pt - 1)
                htr, _ = p.inc("l_tr", 1)

                def t_tr(te, bi=bi, htr=htr):
                    for k in range(DC):
                        inst = te.transpose(bps_x[bi][:, k, :], bxnt[bi][:, 128 * k:128 * (k + 1)],
                                            t["ident"][:, :])
                    inst.then_inc(htr, 1)
                p.emit("tensor", t_tr)
                p.wait("vector", "l_tr", pt + 1)
                hxf, _ = p.inc("l_xf", 1)
                p.emit("vector", lambda v, bi=bi, pt=pt, hxf=hxf: v.tensor_copy(
                    t["xnf1"][:, :, 128 * pt:128 * (pt + 1)], bps_x[bi][:, :, :]).then_inc(hxf, 1))
            p.barrier()

    if upto == "ln":
        _emit_dummy(p, nc, d_y)
        return True

    emit_l1_attention(p, ctx, nc, t, d_wq, d_wk, d_wv, d_wo, d_bo, d_rk, d_rv, d_s1)
    p.barrier()
    if upto == "attn":
        _emit_dummy(p, nc, d_y)
        return True

    t["acc1"] = sb(ctx, nc, "acc1", [128, 16, 1024], F32, side="right")
    t["rw"] = sb(ctx, nc, "rw", [128, 16, 64], F32, side="right")
    hz, _ = p.inc("m_z", 1)
    p.emit("vector", lambda v, hz=hz: v.memset(t["acc1"][:, :, :], 0.0).then_inc(hz, 1))
    with ExitStack() as rctx:
        gwsb = sb(rctx, nc, "gwsb", [128, DC, 64], BF16)
        gbias = sb(rctx, nc, "gbias", [128, 64], F32)
        gbrow = sb(rctx, nc, "gbrow", [1, 64], F32)
        ps_b = ps(rctx, nc, "ps_gb", [128, 64], F32)
        hwg, _ = p.inc("m_dgw", 32)

        def g_gw(g, hwg=hwg):
            g.dma_start(out=gwsb[:, :, :],
                        in_=d_gw.ap().rearrange("(c p) f -> p c f", p=128)).then_inc(hwg, 16)
            g.dma_start(out=gbrow[:, :], in_=d_gb.ap()).then_inc(hwg, 16)
        p.emit("gpsimd", g_gw)
        p.wait("tensor", "m_dgw", 32)
        htb, _ = p.inc("m_tb", 1)
        p.emit("tensor", lambda te, htb=htb: te.matmul(
            ps_b[:, :], t["ones1f"][:, :], gbrow[:, :], start=True, stop=True).then_inc(htb, 1))
        p.wait("vector", "m_tb", 1)
        hvb, _ = p.inc("m_vb", 1)
        p.emit("vector", lambda v, hvb=hvb: v.tensor_copy(gbias[:, :], ps_b[:, :]).then_inc(hvb, 1))
        p.wait("vector", "m_vb", 1)
        emit_routing(p, rctx, nc, t, t["xnf1"], gwsb, gbias, t["rw"])
    p.barrier()
    if upto == "route":
        _emit_dummy(p, nc, d_y)
        return True

    experts = []
    for e in range(n_experts):
        experts.append({
            "wg": d_Wg.ap()[e], "wu": d_Wu.ap()[e],
            "wd_main": d_Wd.ap()[e][0:640, :], "wd_tail": d_Wd.ap()[e][640:704, :],
            "isub": [128] * 5 + [64], "scale": 1.0,
            "rw": (lambda e: (lambda mt: t["rw"][:, mt, e:e + 1]))(e),
        })
    for sh in range(2):
        experts.append({
            "wg": d_Sg.ap()[:, 704 * sh:704 * (sh + 1)],
            "wu": d_Su.ap()[:, 704 * sh:704 * (sh + 1)],
            "wd_main": d_Sd.ap()[704 * sh:704 * sh + 640, :],
            "wd_tail": d_Sd.ap()[704 * sh + 640:704 * (sh + 1), :],
            "isub": [128] * 5 + [64], "scale": 1.0, "rw": None, "wscal": 1.0,
        })
    emit_moe(p, ctx, nc, t["xnf1"], t["acc1"], experts, "m", zwait="m_z")
    p.barrier()
    if upto == "moe":
        _emit_dummy(p, nc, d_y)
        return True

    y_r = d_y.ap().rearrange("(c p) f -> p c f", p=128)
    s1_r = d_s1.ap().rearrange("(c p) f -> p c f", p=128)
    with ExitStack() as ectx:
        gbuf = [sb(ectx, nc, f"f1g{i}", [128, 1024], BF16) for i in range(2)]
        xo2 = [sb(ectx, nc, f"f1x{i}", [128, 1024], F32) for i in range(2)]
        s1b = [sb(ectx, nc, f"f1sb{i}", [128, 1024], BF16) for i in range(2)]
        tbuf = [sb(ectx, nc, f"f1t{i}", [128, 1024], F32) for i in range(2)]
        for pt in range(16):
            bi = pt % 2
            if pt >= 2:
                p.wait("sync", "f1v", pt - 1)
            hx, _ = p.inc(f"f1dx{bi}", 16)

            def s_ld(s, pt=pt, bi=bi, hx=hx):
                s.dma_start(out=xo2[bi][:, :], in_=x1_r[:, pt, :]).then_inc(hx, 8 if False else 16)
            p.emit("sync", s_ld)
            hx2, _ = p.inc(f"f1ds{bi}", 16)
            p.emit("sync", lambda s, pt=pt, bi=bi, hx2=hx2: s.dma_start(
                out=s1b[bi][:, :], in_=s1_r[:, pt, :]).then_inc(hx2, 16))
            if pt >= 2:
                p.wait("scalar", "f1v", pt - 1)
            hg2, _ = p.inc("f1s", 1)
            p.emit("scalar", lambda s, pt=pt, bi=bi, hg2=hg2: s.activation(
                gbuf[bi][:, :], t["acc1"][:, pt, :], AF.Silu).then_inc(hg2, 1))
            p.wait("vector", "f1s", pt + 1)
            p.wait("vector", f"f1dx{bi}")
            p.wait("vector", f"f1ds{bi}")
            if pt >= 2:
                p.wait("vector", f"f1dw{pt % 2}")
            hv2, _ = p.inc("f1v", 1)

            def v_comb(v, pt=pt, bi=bi, hv2=hv2):
                p.rt_chain(v, v.tensor_tensor(
                    tbuf[bi][:, :], s1b[bi][:, :], gbuf[bi][:, :], OP.mult))
                v.tensor_tensor(tbuf[bi][:, :], xo2[bi][:, :], tbuf[bi][:, :], OP.add).then_inc(hv2, 1)
            p.emit("vector", v_comb)
            p.wait("sync", "f1v", pt + 1)
            hw2, _ = p.inc(f"f1dw{pt % 2}", 16)
            p.emit("sync", lambda s, pt=pt, bi=bi, hw2=hw2: s.dma_start(
                out=y_r[:, pt, :], in_=tbuf[bi][:, :]).then_inc(hw2, 16))
        p.wait("sync", "f1dw0")
        p.wait("sync", "f1dw1")
    return False


def emit_l1_attention(p, ctx, nc, t, d_wq, d_wk, d_wv, d_wo, d_bo, d_rk, d_rv, d_s1):
    with ExitStack() as actx:
        y1 = sb(actx, nc, "y1tm", [128, 16, 1024], BF16)
        with ExitStack() as qctx:
            qtm = sb(qctx, nc, "qtm", [128, 16, 1024], BF16)
            ktm = sb(qctx, nc, "ktm", [128, 16, 1024], BF16)
            vtm = sb(qctx, nc, "vtm", [128, 16, 1024], BF16)
            with ExitStack() as wctx:
                wsb = sb(wctx, nc, "l1w", [128, DC, 1024], BF16)
                ps_p = [ps(wctx, nc, f"l1psp{i}", [128, 512], F32) for i in range(2)]
                for wi, (dw, dst) in enumerate([(d_wq, qtm), (d_wk, ktm), (d_wv, vtm)]):
                    if wi >= 1:
                        p.wait("gpsimd", "n_tp", 32 * wi)
                    hwd, _ = p.inc(f"n_dw{wi % 2}", 16)
                    p.emit("gpsimd", lambda g, dw=dw, hwd=hwd: g.dma_start(
                        out=wsb[:, :, :],
                        in_=dw.ap().rearrange("(c p) f -> p c f", p=128)).then_inc(hwd, 16))
                    p.wait("tensor", f"n_dw{wi % 2}")
                    for mt in range(16):
                        for nn in range(2):
                            i2 = 32 * wi + mt * 2 + nn
                            pp = ps_p[i2 % 2]
                            if i2 >= 2:
                                p.wait("tensor", "n_pe", i2 - 1)
                            htp, _ = p.inc("n_tp", 1)

                            def t_p(te, mt=mt, nn=nn, pp=pp, htp=htp):
                                for kc in range(DC):
                                    inst = te.matmul(
                                        pp[:, :], t["xnf1"][:, kc, 128 * mt:128 * (mt + 1)],
                                        wsb[:, kc, 512 * nn:512 * (nn + 1)],
                                        start=(kc == 0), stop=(kc == DC - 1))
                                inst.then_inc(htp, 1)
                            p.emit("tensor", t_p)
                            p.wait("scalar", "n_tp", i2 + 1)
                            hpe, _ = p.inc("n_pe", 1)
                            p.emit("scalar", lambda s, mt=mt, nn=nn, pp=pp, dst=dst, hpe=hpe:
                                   s.activation(dst[:, mt, 512 * nn:512 * (nn + 1)],
                                                pp[:, :], AF.Copy).then_inc(hpe, 1))
                p.barrier()
            with ExitStack() as sctx:
                rkb = sb(sctx, nc, "rkb", [128, B, 1024], BF16)
                rvb = sb(sctx, nc, "rvb", [128, B, 1024], BF16)
                kk = sb(sctx, nc, "kk", [128, 1024], BF16)
                mmb = sb(sctx, nc, "mmb", [128, 1024], F32)
                scoresb = sb(sctx, nc, "scoresb", [128, 8, 16], F32)
                nmax = sb(sctx, nc, "nmax", [128, 1], F32)
                pexp = sb(sctx, nc, "pexp", [128, 8, 16], BF16)
                ssum = sb(sctx, nc, "ssumh", [128, 8], F32)
                srec = sb(sctx, nc, "srech", [128, 8], F32)
                yac = sb(sctx, nc, "yac", [128, 1024], F32)
                n_it = 0
                for s_ in range(B):
                    p.wait("gpsimd", "n_vy", n_it)
                    hrk, _ = p.inc(f"n_drk{s_ % 2}", 32)
                    p.emit("gpsimd", lambda sy, s_=s_, hrk=hrk: (
                        sy.dma_start(out=rkb[:, :, :],
                                     in_=d_rk.ap()[s_].unsqueeze(0).broadcast_to((128, B, D))
                                     ).then_inc(hrk, 16),
                        sy.dma_start(out=rvb[:, :, :],
                                     in_=d_rv.ap()[s_].unsqueeze(0).broadcast_to((128, B, D))
                                     ).then_inc(hrk, 16)))
                    for pt2 in range(2):
                        it = n_it
                        n_it += 1
                        pt = 2 * s_ + pt2
                        p.wait("vector", f"n_drk{s_ % 2}")
                        if it == 0:
                            p.wait("vector", "n_pe", 96)
                        hsc, _ = p.inc("n_sc", 1)

                        def v_scores(v, s_=s_, pt2=pt2, hsc=hsc):
                            C = lambda inst: p.rt_chain(v, inst)
                            for tt in range(B):
                                C(v.tensor_tensor(kk[:, :], ktm[:, 2 * tt + pt2, :],
                                                  rkb[:, tt, :], OP.add))
                                C(v.tensor_tensor(mmb[:, :], qtm[:, 2 * s_ + pt2, :],
                                                  kk[:, :], OP.mult))
                                C(v.tensor_reduce(scoresb[:, tt, :],
                                                  mmb[:, :].rearrange("p (h d) -> p h d", d=HDIM),
                                                  AX.X, OP.add))
                            C(v.tensor_scalar(scoresb[:, :, :], scoresb[:, :, :], QSCALE,
                                              None, OP.mult))
                            v.tensor_reduce(nmax[:, :], scoresb[:, :, :], AX.XY, OP.max,
                                            negate=True).then_inc(hsc, 1)
                        p.emit("vector", v_scores)
                        p.wait("scalar", "n_sc", n_it)
                        hex_, _ = p.inc("n_ex", 1)
                        p.emit("scalar", lambda s, hex_=hex_: s.activation(
                            pexp[:, :, :], scoresb[:, :, :], AF.Exp,
                            bias=nmax[:, :]).then_inc(hex_, 1))
                        p.wait("vector", "n_ex", n_it)
                        hvy, _ = p.inc("n_vy", 1)

                        def v_av(v, s_=s_, pt2=pt2, pt=pt, hvy=hvy):
                            C = lambda inst: p.rt_chain(v, inst)
                            C(v.tensor_reduce(ssum[:, :], pexp[:, :, :], AX.X, OP.add))
                            C(v.reciprocal(srec[:, :], ssum[:, :]))
                            C(v.memset(yac[:, :], 0.0))
                            for tt in range(B):
                                C(v.tensor_scalar(pexp[:, tt, :], pexp[:, tt, :],
                                                  srec[:, tt:tt + 1], None, OP.mult))
                                C(v.tensor_tensor(kk[:, :], vtm[:, 2 * tt + pt2, :],
                                                  rvb[:, tt, :], OP.add))
                                C(v.tensor_tensor(
                                    mmb[:, :].rearrange("p (h d) -> p h d", d=HDIM),
                                    pexp[:, tt, :].unsqueeze(2).broadcast_to((128, 16, 64)),
                                    kk[:, :].rearrange("p (h d) -> p h d", d=HDIM), OP.mult))
                                C(v.tensor_tensor(yac[:, :], yac[:, :], mmb[:, :], OP.add))
                            v.tensor_copy(y1[:, pt, :], yac[:, :]).then_inc(hvy, 1)
                        p.emit("vector", v_av)
                p.wait("vector", "n_vy", n_it)
                p.barrier()
        with ExitStack() as octx:
            y1f = sb(octx, nc, "y1f", [128, DC, TOK], BF16)
            wsb2 = sb(octx, nc, "l1wo", [128, DC, 1024], BF16)
            bobc = sb(octx, nc, "bobc", [128, 1024], F32)
            borow = sb(octx, nc, "borow", [1, 1024], F32)
            s1sb = sb(octx, nc, "s1sb", [128, 2, 1024], BF16)
            ps_t = [ps(octx, nc, f"l1pst{i}", [128, DC, 128], BF16) for i in range(2)]
            ps_p = [ps(octx, nc, f"l1psq{i}", [128, 512], F32) for i in range(2)]
            ps_bo = ps(octx, nc, "ps_bo", [128, 1024], F32)
            hwo, _ = p.inc("n_dwo", 32)

            def g_wo(g, hwo=hwo):
                g.dma_start(out=wsb2[:, :, :],
                            in_=d_wo.ap().rearrange("(c p) f -> p c f", p=128)).then_inc(hwo, 16)
                g.dma_start(out=borow[:, :], in_=d_bo.ap()).then_inc(hwo, 16)
            p.emit("gpsimd", g_wo)
            p.wait("tensor", "n_dwo", 32)
            p.wait("tensor", "v_ini", 1)
            hgb, _ = p.inc("n_tbo", 1)

            def t_bo(te, hgb=hgb):
                te.matmul(ps_bo[:, 0:512], t["ones1f"][:, :], borow[:, 0:512],
                          start=True, stop=True)
                te.matmul(ps_bo[:, 512:1024], t["ones1f"][:, :], borow[:, 512:1024],
                          start=True, stop=True).then_inc(hgb, 1)
            p.emit("tensor", t_bo)
            p.wait("vector", "n_tbo", 1)
            hvbo, _ = p.inc("n_vbo", 1)
            p.emit("vector", lambda v, hvbo=hvbo: v.tensor_copy(
                bobc[:, :], ps_bo[:, :]).then_inc(hvbo, 1))
            for pt in range(16):
                bi = pt % 2
                if pt >= 2:
                    p.wait("tensor", "n_yf", pt - 1)
                hyt, _ = p.inc("n_yt", 1)

                def t_ytr(te, pt=pt, bi=bi, hyt=hyt):
                    for k in range(DC):
                        inst = te.transpose(ps_t[bi][:, k, :], y1[:, pt, 128 * k:128 * (k + 1)],
                                            t["ident"][:, :])
                    inst.then_inc(hyt, 1)
                p.emit("tensor", t_ytr)
                p.wait("vector", "n_yt", pt + 1)
                hyf, _ = p.inc("n_yf", 1)
                p.emit("vector", lambda v, pt=pt, bi=bi, hyf=hyf: v.tensor_copy(
                    y1f[:, :, 128 * pt:128 * (pt + 1)], ps_t[bi][:, :, :]).then_inc(hyf, 1))
            s1_r = d_s1.ap().rearrange("(c p) f -> p c f", p=128)
            p.wait("tensor", "n_yf", 16)
            p.wait("vector", "n_vbo", 1)
            for mt in range(16):
                for nn in range(2):
                    i2 = mt * 2 + nn
                    pp = ps_p[i2 % 2]
                    if i2 >= 2:
                        p.wait("tensor", "n_se", i2 - 1)
                    hto, _ = p.inc("n_to", 1)

                    def t_o(te, mt=mt, nn=nn, pp=pp, hto=hto):
                        for kc in range(DC):
                            inst = te.matmul(pp[:, :], y1f[:, kc, 128 * mt:128 * (mt + 1)],
                                             wsb2[:, kc, 512 * nn:512 * (nn + 1)],
                                             start=(kc == 0), stop=(kc == DC - 1))
                        inst.then_inc(hto, 1)
                    p.emit("tensor", t_o)
                    p.wait("vector", "n_to", i2 + 1)
                    if i2 >= 4 and nn == 0:
                        p.wait("vector", f"n_ds{mt % 2}")
                    hse, _ = p.inc("n_se", 1)
                    p.emit("vector", lambda v, mt=mt, nn=nn, pp=pp, hse=hse: v.tensor_tensor(
                        s1sb[:, mt % 2, 512 * nn:512 * (nn + 1)], pp[:, :],
                        bobc[:, 512 * nn:512 * (nn + 1)], OP.add).then_inc(hse, 1))
                p.wait("sync", "n_se", 2 * mt + 2)
                hds, _ = p.inc(f"n_ds{mt % 2}", 16)
                p.emit("sync", lambda sy, mt=mt, hds=hds: sy.dma_start(
                    out=s1_r[:, mt, :], in_=s1sb[:, mt % 2, :]).then_inc(hds, 16))
            p.wait("sync", "n_ds0")
            p.wait("sync", "n_ds1")




def build_fused(n_experts=MOE_E, nch=32):
    nc = bass.Bass("TRN2", target_bir_lowering=False)
    ctx = ExitStack()
    nc._ctx_keepalive = ctx
    dd0 = _l0_drams(nc, x1_kind="Internal")
    dd1 = _l1_drams(nc, x1_kind=None)
    dd1["x1"] = dd0["x1"]
    p = Prog(nc, ctx)
    t = {}
    with ExitStack() as l0ctx:
        emit_l0_body(p, l0ctx, nc, t, dd0, nch=nch, perm_ctx=ctx)
        p.barrier()
    p.release_sems(keep=("v_ini", "g_ini", "d_w", "c_dw", "f0_dw0", "f0_dw1"))
    p.fresh_names = {"n_dw0", "n_dw1", "n_dwo", "n_drk0", "n_drk1",
                     "m_dgw", "m_dw0", "m_dw1"}
    with ExitStack() as l1ctx:
        emit_l1_body(p, l1ctx, nc, t, dd1, n_experts=n_experts)
    with nc.Block() as block:
        p.build(block)
    return nc


def _prep_l0_inputs(inputs):
    x = np.asarray(inputs["L0_attn_x"], dtype=np.float32).reshape(NTOK, D)
    w_attnT = np.ascontiguousarray(np.asarray(inputs["L0_attn_w_attn"], np.float32).T)
    w_projT = np.ascontiguousarray(np.asarray(inputs["L0_attn_w_proj"], np.float32).T)
    wg0T = np.ascontiguousarray(np.asarray(inputs["L0_ffn_w_gate"], np.int32).T)
    wu0T = np.ascontiguousarray(np.asarray(inputs["L0_ffn_w_up"], np.int32).T)
    wd0T = np.ascontiguousarray(np.asarray(inputs["L0_ffn_w_down"], np.int32).T)
    xr = np.asarray(inputs["L0_attn_x"], np.float32)
    maps = []
    for c in range(NCORES):
        xo = np.ascontiguousarray(xr[:, COLS * c:COLS * (c + 1), :]).reshape(TOK, D)
        maps.append({
            "x_full": x, "x_own": xo,
            "thr": np.array([[COLS * c]], dtype=np.float32),
            "w_attnT": w_attnT, "w_projT": w_projT,
            "wg0T": wg0T, "wu0T": wu0T, "wd0T": wd0T,
        })
    return maps




def _prep_l1_inputs(inputs, x1_full=None):
    f = np.float32
    wqT = np.ascontiguousarray(np.asarray(inputs["L1_attn_w_q"], f).T)
    wkT = np.ascontiguousarray(np.asarray(inputs["L1_attn_w_k"], f).T)
    wvT = np.ascontiguousarray(np.asarray(inputs["L1_attn_w_v"], f).T)
    woT = np.ascontiguousarray(np.asarray(inputs["L1_attn_w_o"], f).T)
    b_o = np.asarray(inputs["L1_attn_b_o"], f).reshape(1, D)
    rel_k = np.asarray(inputs["L1_attn_rel_k"], f)
    rel_v = np.asarray(inputs["L1_attn_rel_v"], f)
    pos = np.arange(B)
    idx = np.clip(pos[:, None] - pos[None, :], -128, 128) + 128
    rk_t = np.ascontiguousarray(np.tile(rel_k[idx], (1, 1, HEADS)))
    rv_t = np.ascontiguousarray(np.tile(rel_v[idx], (1, 1, HEADS)))
    gate_wT = np.ascontiguousarray(np.asarray(inputs["L1_ffn_gate_w"], f).T)
    gate_bias = np.asarray(inputs["L1_ffn_gate_bias"], f).reshape(1, 64)
    WgT = np.ascontiguousarray(np.asarray(inputs["L1_ffn_Wg"], f).transpose(0, 2, 1))
    WuT = np.ascontiguousarray(np.asarray(inputs["L1_ffn_Wu"], f).transpose(0, 2, 1))
    WdT = np.ascontiguousarray(np.asarray(inputs["L1_ffn_Wd"], f).transpose(0, 2, 1))
    SgT = np.ascontiguousarray(np.asarray(inputs["L1_ffn_Sg"], f).T)
    SuT = np.ascontiguousarray(np.asarray(inputs["L1_ffn_Su"], f).T)
    SdT = np.ascontiguousarray(np.asarray(inputs["L1_ffn_Sd"], f).T)
    maps = []
    for c in range(NCORES):
        m = {
            "wqT": wqT, "wkT": wkT, "wvT": wvT, "woT": woT,
            "b_o": b_o, "rk_t": rk_t, "rv_t": rv_t,
            "gate_wT": gate_wT, "gate_bias": gate_bias,
            "WgT": WgT, "WuT": WuT, "WdT": WdT,
            "SgT": SgT, "SuT": SuT, "SdT": SdT,
        }
        if x1_full is not None:
            m["x1_own"] = np.ascontiguousarray(
                x1_full[:, COLS * c:COLS * (c + 1), :]).reshape(TOK, D)
        maps.append(m)
    return maps


def kernel(**inputs):
    nc = build_fused()
    maps0 = _prep_l0_inputs(inputs)
    maps1 = _prep_l1_inputs(inputs, None)
    maps = [dict(m0, **m1) for m0, m1 in zip(maps0, maps1)]
    res = run_bass_kernel_spmd(nc, maps, core_ids=list(range(NCORES)))
    y = np.concatenate(
        [np.asarray(r["y_own"]).reshape(B, COLS, D) for r in res.results], axis=1)
    return y



# version 12
# speedup vs baseline: 1.0613x; 1.0613x over previous
import math
from contextlib import ExitStack

import numpy as np
import concourse.bass as bass
import concourse.mybir as mybir
from concourse.bass_utils import run_bass_kernel_spmd

F32 = mybir.dt.float32
BF16 = mybir.dt.bfloat16
I32 = mybir.dt.int32
AF = mybir.ActivationFunctionType
OP = mybir.AluOpType
AX = mybir.AxisListType

NCORES = 8
B, T, D = 8, 2048, 1024
COLS = T // NCORES
TOK = B * COLS
NTOK = B * T
DC = D // 128
HEADS, HDIM = 16, 64
QSCALE = 1.0 / math.sqrt(HDIM)
DEQ = 1.0 / 127.0
EPS = 1e-5
NEG = -1e30
ALIBI = [2.0 ** (-0.5 * (i + 1)) for i in range(HEADS)]
MOE_E, MOE_I, S_I = 64, 704, 1408


class Prog:

    ENGINES = ("sync", "tensor", "vector", "scalar", "gpsimd")

    def __init__(self, nc, ctx):
        self.nc = nc
        self.ctx = ctx
        self.ops = {e: [] for e in self.ENGINES}
        self._sems = {}
        self._barsc = ctx.enter_context(
            nc.sbuf_tensor("barsc", [128, 4], mybir.dt.float32))

    def _s(self, name):
        if name not in self._sems:
            pool = getattr(self, "_pool", None)
            if name in getattr(self, "fresh_names", ()):
                pool = None
            if pool:
                h, cnt = pool.pop()
                self._sems[name] = [h, cnt, cnt]
            else:
                h = self.ctx.enter_context(self.nc.semaphore(name))
                self._sems[name] = [h, 0, 0]
        return self._sems[name]

    def rel(self, name):
        s = self._s(name)
        return s[1] - s[2]

    def release_sems(self, keep=()):
        if not hasattr(self, "_pool"):
            self._pool = []
        for name in list(self._sems):
            if name.startswith("bar_") or name in keep:
                continue
            s = self._sems.pop(name)
            self._pool.append((s[0], s[1]))

    def inc(self, name, n=1):
        s = self._s(name)
        s[1] += n
        return s[0], s[1]

    def val(self, name):
        return self._s(name)[1]

    def rt_chain(self, eng, inst):
        if inst is None:
            return None
        key = str(getattr(eng, "engine", getattr(eng, "engine_type", type(eng).__name__)))
        if not hasattr(self, "_chain"):
            self._chain = {}
        if key not in self._chain:
            self._chain[key] = [self.ctx.enter_context(self.nc.semaphore(f"chain_{len(self._chain)}")), 0]
        c = self._chain[key]
        c[1] += 1
        inst.then_inc(c[0], 1)
        eng.wait_ge(c[0], c[1])
        return inst

    def emit(self, engine, fn):
        d = getattr(self, "_defer", None)
        if d is not None:
            d[engine].append(fn)
        else:
            self.ops[engine].append(fn)

    def defer_start(self):
        self._defer = {e: [] for e in self.ENGINES}

    def defer_end(self):
        d = self._defer
        self._defer = None
        return d

    def emit_deferred(self, d):
        if d:
            for e, fns in d.items():
                self.ops[e].extend(fns)

    def wait(self, engine, name, val=None):
        s = self._s(name)
        v = s[1] if val is None else s[2] + val
        if v <= 0:
            return
        self.emit(engine, lambda e, h=s[0], v=v: e.wait_ge(h, v))

    def barrier(self, engines=None):
        engines = list(engines or self.ENGINES)
        targets = {}
        col = {"vector": 0, "scalar": 1, "gpsimd": 2}
        for e in engines:
            h, v = self.inc(f"bar_{e}", 1)
            targets[e] = (h, v)
            if e == "vector":
                self.emit(e, lambda eng, h=h, v=v: (eng.memset(
                    self._barsc[:, 0:1], 0.0).then_inc(h, 1), eng.wait_ge(h, v)))
            elif e == "gpsimd":
                self.emit(e, lambda eng, h=h, v=v: (eng.memset(
                    self._barsc[:, 2:3], 0.0).then_inc(h, 1), eng.wait_ge(h, v)))
            elif e == "scalar":
                self.emit(e, lambda eng, h=h, v=v: (eng.memzero(
                    self._barsc[:, 1:2]).then_inc(h, 1), eng.wait_ge(h, v)))
            else:
                self.emit(e, lambda eng, h=h: eng.nop().then_inc(h, 1))
        for e in engines:
            for o in engines:
                if o != e:
                    h, v = targets[o]
                    self.emit(e, lambda eng, h=h, v=v: eng.wait_ge(h, v))

    def build(self, block):
        for e in self.ENGINES:
            ops = self.ops[e]
            getattr(block, e)(lambda eng, ops=ops: [op(eng) for op in ops])


def sb(ctx, nc, name, shape, dtype, side=None):
    return ctx.enter_context(nc.sbuf_tensor(name, shape, dtype, side=side))


def ps(ctx, nc, name, shape, dtype=F32):
    return ctx.enter_context(nc.psum_tensor(name, shape, dtype))


def emit_ln_tile(p, x_ap, out_ap, sc, done_sem, wait_before=None):
    s_sq, _ = p.inc(done_sem + "_sq", 1)
    v_sq = p.rel(done_sem + "_sq")
    if wait_before:
        p.wait("scalar", *wait_before)
    p.emit("scalar", lambda s, x=x_ap: s.activation(
        sc["sq"][:, :], x, AF.Square, accum_out=sc["sqs"][:, :]).then_inc(s_sq, 1))
    p.wait("vector", done_sem + "_sq", v_sq)
    s_st, _ = p.inc(done_sem + "_st", 1)
    v_st = p.rel(done_sem + "_st")

    def v_stats(v, x=x_ap):
        c0 = p.rt_chain(v, None)
        c0 = p.rt_chain(v, v.tensor_reduce(sc["ssum"][:, :], x, AX.X, OP.add))
        c0 = p.rt_chain(v, v.tensor_scalar(sc["mean"][:, :], sc["ssum"][:, :], 1.0 / D, None, OP.mult))
        c0 = p.rt_chain(v, v.tensor_tensor(sc["var"][:, :], sc["mean"][:, :], sc["mean"][:, :], OP.mult))
        c0 = p.rt_chain(v, v.tensor_scalar(sc["sqs"][:, :], sc["sqs"][:, :], 1.0 / D, None, OP.mult))
        c0 = p.rt_chain(v, v.tensor_tensor(sc["var"][:, :], sc["sqs"][:, :], sc["var"][:, :], OP.subtract))
        v.tensor_scalar(sc["var"][:, :], sc["var"][:, :], EPS, None, OP.add).then_inc(s_st, 1)
    p.emit("vector", v_stats)
    p.wait("scalar", done_sem + "_st", v_st)
    s_rs, _ = p.inc(done_sem + "_rs", 1)
    v_rs = p.rel(done_sem + "_rs")

    def s_rstd(s):
        p.rt_chain(s, s.activation(sc["var"][:, :], sc["var"][:, :], AF.Ln))
        s.activation(sc["rstd"][:, :], sc["var"][:, :], AF.Exp, scale=-0.5).then_inc(s_rs, 1)
    p.emit("scalar", s_rstd)
    p.wait("vector", done_sem + "_rs", v_rs)
    hd, _ = p.inc(done_sem, 1)
    p.emit("vector", lambda v, x=x_ap, o=out_ap: v.tensor_scalar(
        o, x, sc["mean"][:, :], sc["rstd"][:, :], OP.subtract, OP.mult).then_inc(hd, 1))


def make_ln_scratch(ctx, nc, prefix):
    sc = {"sq": sb(ctx, nc, prefix + "sq", [128, 1024], F32)}
    for n in ("ssum", "mean", "sqs", "var", "rstd"):
        sc[n] = sb(ctx, nc, prefix + n, [128, 1], F32)
    return sc



def _emit_dummy(p, nc, d_out):
    with nc.sbuf_tensor("zz", [128, 16, 1024], F32) as zz:
        hz, _ = p.inc("zz", 1)
        p.emit("gpsimd", lambda g, hz=hz: g.memset(zz[:, :, :], 0.0).then_inc(hz, 1))
        p.wait("sync", "zz")
        hd, _ = p.inc("d_x1d", 16)
        p.emit("sync", lambda s, hd=hd: s.dma_start(
            out=d_out.ap().rearrange("(c p) f -> p c f", p=128), in_=zz[:, :, :]).then_inc(hd, 16))
        p.wait("sync", "d_x1d")


def _l0_drams(nc, x1_kind="ExternalOutput", kv_kind="Internal"):
    dd = {}
    dd["xf"] = nc.dram_tensor("x_full", [NTOK, D], F32, kind="ExternalInput")
    dd["xo"] = nc.dram_tensor("x_own", [TOK, D], F32, kind="ExternalInput")
    dd["thr"] = nc.dram_tensor("thr", [1, 1], F32, kind="ExternalInput")
    dd["wattnT"] = nc.dram_tensor("w_attnT", [D, 3 * D], F32, kind="ExternalInput")
    dd["wprojT"] = nc.dram_tensor("w_projT", [D, D], F32, kind="ExternalInput")
    dd["wg0T"] = nc.dram_tensor("wg0T", [D, 3072], I32, kind="ExternalInput")
    dd["wu0T"] = nc.dram_tensor("wu0T", [D, 3072], I32, kind="ExternalInput")
    dd["wd0T"] = nc.dram_tensor("wd0T", [3072, D], I32, kind="ExternalInput")
    dd["x1"] = nc.dram_tensor("x1_own", [TOK, D], F32, kind=x1_kind)
    dd["ksc"] = nc.dram_tensor("Ksc", [D, NTOK], BF16, kind=kv_kind)
    dd["vsc"] = nc.dram_tensor("Vsc", [D, NTOK], BF16, kind=kv_kind)
    return dd


def build_l0(debug_kv=False, upto="full", skip_a=False, kv_ext=False, nch=32):
    nc = bass.Bass("TRN2", target_bir_lowering=False)
    ctx = ExitStack()
    nc._ctx_keepalive = ctx
    dd = _l0_drams(nc, kv_kind="ExternalOutput" if (debug_kv or kv_ext) else "Internal")
    p = Prog(nc, ctx)
    t = {}
    emit_l0_body(p, ctx, nc, t, dd, debug_kv, upto, skip_a, nch)
    with nc.Block() as block:
        p.build(block)
    return nc


def emit_l0_body(p, ctx, nc, t, dd, debug_kv=False, upto="full", skip_a=False, nch=32,
                 perm_ctx=None):
    d_xf, d_xo, d_thr = dd["xf"], dd["xo"], dd["thr"]
    d_wattnT, d_wprojT = dd["wattnT"], dd["wprojT"]
    d_wg0T, d_wu0T, d_wd0T = dd["wg0T"], dd["wu0T"], dd["wd0T"]
    d_x1, d_ksc, d_vsc = dd["x1"], dd["ksc"], dd["vsc"]

    pctx = perm_ctx if perm_ctx is not None else ctx
    t["ident"] = sb(pctx, nc, "ident", [128, 128], BF16)
    t["ones1"] = sb(pctx, nc, "ones1", [1, 128], BF16)
    t["ones1f"] = sb(pctx, nc, "ones1f", [1, 128], F32)
    t["thr1"] = sb(ctx, nc, "thr1", [1, 1], F32)
    t["thrc"] = sb(ctx, nc, "thrc", [128, 1], F32)
    era1 = ExitStack()
    t["kcol"] = sb(era1, nc, "kcol", [128, T], F32)
    t["mask"] = sb(era1, nc, "mask", [128, 2, T], BF16)
    t["wq"] = sb(era1, nc, "wq", [128, DC, D], BF16)

    with nc.sbuf_tensor("dmp", [128, T], F32) as dmp, \
         nc.psum_tensor("ps_i", [128, 1], F32) as ps_i:
        hg, _ = p.inc("g_ini", 1)

        def g_init(g, hg=hg):
            g.iota(dmp[:, :], [[1, T]], base=0, channel_multiplier=-1,
                   allow_small_or_imprecise_dtypes=True)
            g.iota(t["kcol"][:, :], [[1, T]], base=0, channel_multiplier=0,
                   allow_small_or_imprecise_dtypes=True)
            g.memset(t["ones1"][:, :], 1.0)
            g.memset(t["ones1f"][:, :], 1.0).then_inc(hg, 1)
        p.emit("gpsimd", g_init)
        hs, _ = p.inc("d_thr", 16)
        p.emit("sync", lambda s, hs=hs: s.dma_start(
            out=t["thr1"][:, :], in_=d_thr.ap()).then_inc(hs, 16))
        p.wait("vector", "g_ini")
        hv, _ = p.inc("v_ini", 1)
        p.emit("vector", lambda v, hv=hv: v.tensor_scalar(
            t["ident"][:, :], dmp[:, 0:128], 0.0, None, OP.is_equal).then_inc(hv, 1))
        p.wait("tensor", "v_ini")
        p.wait("tensor", "d_thr")
        ht, _ = p.inc("t_ini", 1)
        p.emit("tensor", lambda te, ht=ht: te.matmul(
            ps_i[:, :], t["ones1f"][:, :], t["thr1"][:, :],
            start=True, stop=True).then_inc(ht, 1))
        p.wait("vector", "t_ini")
        hv2, _ = p.inc("v_msk", 1)

        def v_mask(v, hv2=hv2):
            p.rt_chain(v, v.tensor_copy(t["thrc"][:, :], ps_i[:, :]))
            p.rt_chain(v, v.tensor_scalar(t["mask"][:, 0, :], dmp[:, :], t["thrc"][:, :], NEG,
                                          OP.is_gt, OP.mult))
            p.rt_chain(v, v.tensor_scalar(dmp[:, :], dmp[:, :], 128.0, None, OP.subtract))
            v.tensor_scalar(t["mask"][:, 1, :], dmp[:, :], t["thrc"][:, :], NEG,
                            OP.is_gt, OP.mult).then_inc(hv2, 1)
        p.emit("vector", v_mask)

        hw, _ = p.inc("d_w", 16)
        p.emit("gpsimd", lambda g, hw=hw: g.dma_start(
            out=t["wq"][:, :, :],
            in_=d_wattnT.ap().rearrange("(c p) f -> p c f", p=128)[:, :, 0:D]
        ).then_inc(hw, 16))

        with ExitStack() as actx:
            NCH = 0 if skip_a else nch
            wkv = sb(actx, nc, "wkv", [128, DC, 2 * D], BF16)
            hwk, _ = p.inc("d_w", 16)
            p.emit("gpsimd", lambda g, hwk=hwk: g.dma_start(
                out=wkv[:, :, :],
                in_=d_wattnT.ap().rearrange("(c p) f -> p c f", p=128)[:, :, D:3 * D]
            ).then_inc(hwk, 16))
            xin = [sb(actx, nc, f"xin{i}", [128, 4, 1024], F32) for i in range(2)]
            xnt = [sb(actx, nc, f"xnt{i}", [128, 4, 1024], BF16) for i in range(2)]
            xnf = [sb(actx, nc, f"xnf{i}", [128, DC, 512], BF16) for i in range(2)]
            kbuf = [sb(actx, nc, f"kbuf{i}", [128, DC, 512], BF16) for i in range(2)]
            vbuf = [sb(actx, nc, f"vbuf{i}", [128, DC, 512], BF16) for i in range(2)]
            lns = make_ln_scratch(actx, nc, "lnA")
            ps_x = [ps(actx, nc, f"ps_x{i}", [128, DC, 128], BF16) for i in range(2)]
            ps_k = [ps(actx, nc, f"ps_k{i}", [128, 512], F32) for i in range(2)]
            ps_v = [ps(actx, nc, f"ps_v{i}", [128, 512], F32) for i in range(2)]

            xf_r = d_xf.ap().rearrange("(c p) f -> p c f", p=128)
            ksc_r = d_ksc.ap().rearrange("(c p) t -> p c t", p=128)
            vsc_r = d_vsc.ap().rearrange("(c p) t -> p c t", p=128)

            for ci in range(NCH):
                bi = ci % 2
                if ci >= 2:
                    p.wait("sync", "a_ln", 4 * (ci - 1))
                hx, _ = p.inc(f"a_dx{bi}", 16)
                p.emit("sync", lambda s, ci=ci, bi=bi, hx=hx: s.dma_start(
                    out=xin[bi][:, :, :], in_=xf_r[:, 4 * ci:4 * ci + 4, :]).then_inc(hx, 16))
                if ci >= 2:
                    p.wait("vector", "a_tr", 4 * ci - 4)
                for j in range(4):
                    emit_ln_tile(p, xin[bi][:, j, :], xnt[bi][:, j, :], lns, "a_ln",
                                 wait_before=(f"a_dx{bi}", None) if j == 0 else None)
                for j in range(4):
                    pj = ps_x[j % 2]
                    if ci > 0 or j >= 2:
                        p.wait("tensor", "a_xf", 4 * ci + j - 1)
                    p.wait("tensor", "a_ln", 4 * ci + j + 1)
                    htr, _ = p.inc("a_tr", 1)

                    def t_tr(te, bi=bi, j=j, pj=pj, htr=htr):
                        for k in range(DC):
                            inst = te.transpose(pj[:, k, :], xnt[bi][:, j, 128 * k:128 * (k + 1)],
                                                t["ident"][:, :])
                        inst.then_inc(htr, 1)
                    p.emit("tensor", t_tr)
                    p.wait("vector", "a_tr", 4 * ci + j + 1)
                    if ci >= 2 and j == 0:
                        p.wait("vector", "a_tk", 8 * (ci - 1))
                        p.wait("vector", "a_tv", 8 * (ci - 1))
                    hxf, _ = p.inc("a_xf", 1)
                    p.emit("vector", lambda v, bi=bi, j=j, pj=pj, hxf=hxf: v.tensor_copy(
                        xnf[bi][:, :, 128 * j:128 * (j + 1)], pj[:, :, :]).then_inc(hxf, 1))
                p.wait("tensor", "a_xf", 4 * (ci + 1))
                if ci == 0:
                    p.wait("tensor", "d_w", 32)
                for mc in range(DC):
                    pk = ps_k[mc % 2]
                    if ci > 0 or mc >= 2:
                        p.wait("tensor", "a_ke", 8 * ci + mc - 1)
                    htk, _ = p.inc("a_tk", 1)

                    def t_k(te, bi=bi, mc=mc, pk=pk, htk=htk):
                        for kc in range(DC):
                            inst = te.matmul(pk[:, :], wkv[:, kc, 128 * mc:128 * (mc + 1)],
                                             xnf[bi][:, kc, :], start=(kc == 0), stop=(kc == DC - 1))
                        inst.then_inc(htk, 1)
                    p.emit("tensor", t_k)
                    p.wait("scalar", "a_tk", 8 * ci + mc + 1)
                    if ci >= 2 and mc == 0:
                        p.wait("scalar", f"a_dk{ci % 2}")
                    hke, _ = p.inc("a_ke", 1)
                    p.emit("scalar", lambda s, bi=bi, mc=mc, pk=pk, hke=hke: s.activation(
                        kbuf[bi][:, mc, :], pk[:, :], AF.Copy).then_inc(hke, 1))
                for mc in range(DC):
                    pv = ps_v[mc % 2]
                    if ci > 0 or mc >= 2:
                        p.wait("tensor", "a_ve", 8 * ci + mc - 1)
                    htv, _ = p.inc("a_tv", 1)

                    def t_v(te, bi=bi, mc=mc, pv=pv, htv=htv):
                        for kc in range(DC):
                            inst = te.matmul(pv[:, :], wkv[:, kc, D + 128 * mc:D + 128 * (mc + 1)],
                                             xnf[bi][:, kc, :], start=(kc == 0), stop=(kc == DC - 1))
                        inst.then_inc(htv, 1)
                    p.emit("tensor", t_v)
                    p.wait("vector", "a_tv", 8 * ci + mc + 1)
                    if ci >= 2 and mc == 0:
                        p.wait("vector", f"a_dv{ci % 2}")
                    hve, _ = p.inc("a_ve", 1)
                    p.emit("vector", lambda v, bi=bi, mc=mc, pv=pv, hve=hve: v.tensor_copy(
                        vbuf[bi][:, mc, :], pv[:, :]).then_inc(hve, 1))
                p.wait("sync", "a_ke", 8 * (ci + 1))
                hdk, _ = p.inc(f"a_dk{ci % 2}", 16)
                p.emit("sync", lambda s, bi=bi, ci=ci, hdk=hdk: s.dma_start(
                    out=ksc_r[:, :, 512 * ci:512 * (ci + 1)], in_=kbuf[bi][:, :, :]).then_inc(hdk, 16))
                p.wait("sync", "a_ve", 8 * (ci + 1))
                hdv, _ = p.inc(f"a_dv{ci % 2}", 16)
                p.emit("sync", lambda s, bi=bi, ci=ci, hdv=hdv: s.dma_start(
                    out=vsc_r[:, :, 512 * ci:512 * (ci + 1)], in_=vbuf[bi][:, :, :]).then_inc(hdv, 16))

            if NCH:
                for par in range(2):
                    p.wait("sync", f"a_dk{par}")
                    p.wait("sync", f"a_dv{par}")
            p.barrier()

    if debug_kv:
        with nc.sbuf_tensor("zz", [128, 16, 1024], F32) as zz:
            hz, _ = p.inc("zz", 1)
            p.emit("gpsimd", lambda g: g.memset(zz[:, :, :], 0.0).then_inc(hz, 1))
            p.wait("sync", "zz")
            hd, _ = p.inc("d_x1", 16)
            p.emit("sync", lambda s: s.dma_start(
                out=d_x1.ap().rearrange("(c p) f -> p c f", p=128), in_=zz[:, :, :]).then_inc(hd, 16))
            p.wait("sync", "d_x1")
        return True

    t["xnf"] = sb(ctx, nc, "xnf_own", [128, DC, TOK], BF16, side="right")
    t["yfm"] = sb(era1, nc, "yfm", [128, DC, TOK], BF16)
    xo_r = d_xo.ap().rearrange("(c p) f -> p c f", p=128)
    with ExitStack() as bctx:
        bxin = [sb(bctx, nc, f"b_bxin{i}", [128, 1024], F32) for i in range(2)]
        bxnt = [sb(bctx, nc, f"b_bxnt{i}", [128, 1024], BF16) for i in range(2)]
        lns = make_ln_scratch(bctx, nc, "lnB")
        bps_x = [ps(bctx, nc, f"b_psx{i}", [128, DC, 128], BF16) for i in range(2)]
        for pt in range(16):
            bi = pt % 2
            if pt >= 2:
                p.wait("sync", "b_ln", pt - 1)
            hx, _ = p.inc(f"b_dx{bi}", 16)
            p.emit("sync", lambda s, pt=pt, bi=bi, hx=hx: s.dma_start(
                out=bxin[bi][:, :], in_=xo_r[:, pt, :]).then_inc(hx, 16))
            if pt >= 2:
                p.wait("vector", "b_tr", pt - 1)
            emit_ln_tile(p, bxin[bi][:, :], bxnt[bi][:, :], lns, "b_ln",
                         wait_before=(f"b_dx{bi}", None))
            p.wait("tensor", "b_ln", pt + 1)
            if pt >= 2:
                p.wait("tensor", "b_xf", pt - 1)
            htr, _ = p.inc("b_tr", 1)

            def t_tr(te, bi=bi, htr=htr):
                for k in range(DC):
                    inst = te.transpose(bps_x[bi][:, k, :], bxnt[bi][:, 128 * k:128 * (k + 1)],
                                        t["ident"][:, :])
                inst.then_inc(htr, 1)
            p.emit("tensor", t_tr)
            p.wait("vector", "b_tr", pt + 1)
            hxf, _ = p.inc("b_xf", 1)
            p.emit("vector", lambda v, bi=bi, pt=pt, hxf=hxf: v.tensor_copy(
                t["xnf"][:, :, 128 * pt:128 * (pt + 1)], bps_x[bi][:, :, :]).then_inc(hxf, 1))
        p.barrier()

    if upto == "b0":
        _emit_dummy(p, nc, d_x1)
        return True

    emit_l0_attention(p, ctx, nc, t, d_ksc, d_vsc)
    p.barrier()

    if upto == "attn":
        _emit_dummy(p, nc, d_x1)
        return True

    t["s0"] = sb(ctx, nc, "s0", [128, 16, 1024], BF16, side="right")
    with ExitStack() as cctx:
        wproj = sb(cctx, nc, "wproj", [128, DC, D], BF16)
        hwp, _ = p.inc("c_dw", 16)
        p.emit("gpsimd", lambda g, hwp=hwp: g.dma_start(
            out=wproj[:, :, :],
            in_=d_wprojT.ap().rearrange("(c p) f -> p c f", p=128)).then_inc(hwp, 16))
        p.wait("tensor", "c_dw", 16)
        ps_p = [ps(cctx, nc, f"c_ps{i}", [128, 512], F32) for i in range(2)]
        for mt in range(16):
            for nn in range(2):
                i2 = mt * 2 + nn
                pp = ps_p[i2 % 2]
                if i2 >= 2:
                    p.wait("tensor", "c_pe", i2 - 1)
                htp, _ = p.inc("c_tp", 1)

                def t_p(te, mt=mt, nn=nn, pp=pp, htp=htp):
                    for kc in range(DC):
                        inst = te.matmul(pp[:, :], t["yfm"][:, kc, 128 * mt:128 * (mt + 1)],
                                         wproj[:, kc, 512 * nn:512 * (nn + 1)],
                                         start=(kc == 0), stop=(kc == DC - 1))
                    inst.then_inc(htp, 1)
                p.emit("tensor", t_p)
                p.wait("scalar", "c_tp", i2 + 1)
                hpe, _ = p.inc("c_pe", 1)
                p.emit("scalar", lambda s, mt=mt, nn=nn, pp=pp, hpe=hpe: s.activation(
                    t["s0"][:, mt, 512 * nn:512 * (nn + 1)], pp[:, :], AF.Copy).then_inc(hpe, 1))
        p.barrier()

    if upto == "c":
        era1.close()
        _finish_dummy(p, nc, d_x1)
        return nc

    era1.close()

    t["acc"] = sb(ctx, nc, "acc", [128, 16, 1024], F32, side="right")
    hz, _ = p.inc("d_z", 1)
    p.emit("vector", lambda v, hz=hz: v.memset(t["acc"][:, :, :], 0.0).then_inc(hz, 1))
    emit_ffn_chunked(
        p, ctx, nc, t,
        xn_fm=t["xnf"], acc=t["acc"],
        wg_dram=d_wg0T, wu_dram=d_wu0T, wd_dram=d_wd0T,
        n_ic=6, isz=512, silu_scale=DEQ, wacc_scalar=DEQ * DEQ,
        rw=None, tag="f0", zwait="d_z",
    )
    p.barrier()

    if upto == "d":
        _emit_dummy(p, nc, d_x1)
        return True

    x1_r = d_x1.ap().rearrange("(c p) f -> p c f", p=128)
    with ExitStack() as ectx:
        gbuf = [sb(ectx, nc, f"e_g{i}", [128, 1024], BF16) for i in range(2)]
        xo2 = [sb(ectx, nc, f"e_x{i}", [128, 1024], F32) for i in range(2)]
        tbuf = [sb(ectx, nc, f"e_t{i}", [128, 1024], F32) for i in range(2)]
        for pt in range(16):
            bi = pt % 2
            if pt >= 2:
                p.wait("sync", "e_v", pt - 1)
            hx, _ = p.inc(f"e_dx{bi}", 16)
            p.emit("sync", lambda s, pt=pt, bi=bi, hx=hx: s.dma_start(
                out=xo2[bi][:, :], in_=xo_r[:, pt, :]).then_inc(hx, 16))
            if pt >= 2:
                p.wait("scalar", "e_v", pt - 1)
            hg, _ = p.inc("e_s", 1)
            p.emit("scalar", lambda s, pt=pt, bi=bi, hg=hg: s.activation(
                gbuf[bi][:, :], t["acc"][:, pt, :], AF.Silu).then_inc(hg, 1))
            p.wait("vector", "e_s", pt + 1)
            p.wait("vector", f"e_dx{bi}")
            if pt >= 2:
                p.wait("vector", f"e_dw{pt % 2}")
            hv, _ = p.inc("e_v", 1)

            def v_comb(v, pt=pt, bi=bi, hv=hv):
                p.rt_chain(v, v.tensor_tensor(
                    tbuf[bi][:, :], t["s0"][:, pt, :], gbuf[bi][:, :], OP.mult))
                v.tensor_tensor(tbuf[bi][:, :], xo2[bi][:, :], tbuf[bi][:, :], OP.add).then_inc(hv, 1)
            p.emit("vector", v_comb)
            p.wait("sync", "e_v", pt + 1)
            hw, _ = p.inc(f"e_dw{pt % 2}", 16)
            p.emit("sync", lambda s, pt=pt, bi=bi, hw=hw: s.dma_start(
                out=x1_r[:, pt, :], in_=tbuf[bi][:, :]).then_inc(hw, 16))
        p.wait("sync", "e_dw0")
        p.wait("sync", "e_dw1")
    return False


def emit_l0_attention(p, ctx, nc, t, d_ksc, d_vsc):
    with ExitStack() as actx:
        kh = sb(actx, nc, "kh", [128, NTOK // 2], BF16)
        vf = sb(actx, nc, "vf", [128, NTOK // 2], BF16)
        vt = sb(actx, nc, "vt", [128, 64, 128], BF16)
        qh = sb(actx, nc, "qh", [128, TOK], BF16)
        bias = sb(actx, nc, "bias", [128, 2, T], F32)
        bb = [sb(actx, nc, f"bb{i}", [128, T], F32) for i in range(2)]
        pb = [sb(actx, nc, f"pb{i}", [128, T], BF16) for i in range(3)]
        at_sb = sb(actx, nc, "at_sb", [128, 16, 128], BF16)
        rmax = [sb(actx, nc, f"rmax{i}", [128, 1], F32) for i in range(2)]
        rsum = [sb(actx, nc, f"rsum{i}", [128, 1], F32) for i in range(3)]
        rrec = [sb(actx, nc, f"rrec{i}", [128, 1], F32) for i in range(3)]
        ps_q = [ps(actx, nc, f"a_psq{i}", [128, 512], F32) for i in range(2)]
        ps_sc = [ps(actx, nc, f"a_pssc{i}", [128, 512], F32) for i in range(2)]
        ps_at = [ps(actx, nc, f"a_psat{i}", [128, 8, 128], BF16) for i in range(2)]
        ps_av = ps(actx, nc, "a_psav", [128, 128], F32)
        ps_vt = ps(actx, nc, "a_psvt", [128, DC, 128], BF16)

        ksc_r = d_ksc.ap().rearrange("(c p) t -> p c t", p=128)
        vsc_r = d_vsc.ap().rearrange("(c p) t -> p c t", p=128)

        n_it = 0
        n_sc = 0
        n_tt = 0
        n_vt = 0
        pending = []

        def flush():
            while pending:
                p.emit_deferred(pending.pop(0))
        for hc in range(DC):
            if hc > 0:
                p.wait("scalar", "h_ts", n_sc)
            for nc4 in range(4):
                pq = ps_q[nc4 % 2]
                if hc > 0 or nc4 >= 2:
                    p.wait("tensor", "h_qe", 4 * hc + nc4 - 1)
                htq, _ = p.inc("h_tq", 1)

                def t_q(te, hc=hc, nc4=nc4, pq=pq, htq=htq):
                    for kc in range(DC):
                        inst = te.matmul(pq[:, :], t["wq"][:, kc, 128 * hc:128 * (hc + 1)],
                                         t["xnf"][:, kc, 512 * nc4:512 * (nc4 + 1)],
                                         start=(kc == 0), stop=(kc == DC - 1))
                    inst.then_inc(htq, 1)
                p.emit("tensor", t_q)
                p.wait("scalar", "h_tq", 4 * hc + nc4 + 1)
                hqe, _ = p.inc("h_qe", 1)
                p.emit("scalar", lambda s, nc4=nc4, pq=pq, hqe=hqe: s.activation(
                    qh[:, 512 * nc4:512 * (nc4 + 1)], pq[:, :], AF.Copy).then_inc(hqe, 1))
            for half in range(2):
                hidx = 2 * hc + half
                flush()
                if hidx >= 1:
                    p.wait("sync", "h_ts", 64 * hidx)
                    p.wait("sync", "h_tv", 8 * hidx)
                hk, _ = p.inc("h_dk", 32)
                p.emit("sync", lambda s, hc=hc, half=half, hk=hk: (
                    s.dma_start(out=kh[:, :], in_=ksc_r[:, hc, 8192 * half:8192 * (half + 1)]
                                ).then_inc(hk, 16),
                    s.dma_start(out=vf[:, :], in_=vsc_r[:, hc, 8192 * half:8192 * (half + 1)]
                                ).then_inc(hk, 16)))
                p.wait("tensor", "h_dk", 32 * (hidx + 1))
                if hidx >= 1:
                    p.wait("vector", "h_tav", n_it)
                for g8 in range(8):
                    n_vt += 1
                    if n_vt >= 2:
                        p.wait("tensor", "h_ve", n_vt - 1)
                    hvt, _ = p.inc("h_tv", 1)

                    def t_vt(te, g8=g8, hvt=hvt):
                        for k in range(8):
                            kt = 8 * g8 + k
                            inst = te.transpose(ps_vt[:, k, :], vf[:, 128 * kt:128 * (kt + 1)],
                                                t["ident"][:, :])
                        inst.then_inc(hvt, 1)
                    p.emit("tensor", t_vt)
                    p.wait("vector", "h_tv", n_vt)
                    hve, _ = p.inc("h_ve", 1)
                    p.emit("vector", lambda v, g8=g8, hve=hve: v.tensor_copy(
                        vt[:, 8 * g8:8 * (g8 + 1), :], ps_vt[:, :, :]).then_inc(hve, 1))
                for h2 in range(2):
                    h = 2 * hc + h2
                    slope = ALIBI[h]
                    p.wait("vector", "v_msk", 1)
                    if n_it > 0:
                        p.wait("vector", "h_se", n_it)
                    hbi, _ = p.inc("h_bias", 1)
                    nbias = p.rel("h_bias")

                    def v_bias(v, slope=slope, hbi=hbi):
                        v.scalar_tensor_tensor(bias[:, 0, :], t["kcol"][:, :], slope,
                                               t["mask"][:, 0, :], OP.mult, OP.add)
                        v.scalar_tensor_tensor(bias[:, 1, :], t["kcol"][:, :], slope,
                                               t["mask"][:, 1, :], OP.mult, OP.add).then_inc(hbi, 1)
                    p.emit("vector", v_bias)
                    for r4 in range(4):
                        r = 4 * half + r4
                        for qt in range(2):
                            it = n_it
                            n_it += 1
                            bbuf = bb[it % 2]
                            pbuf = pb[it % 3]
                            q0 = 256 * r + 128 * qt
                            k0 = 2048 * r4
                            if it == 0 or (half == 0 and h2 == 0 and r4 == 0 and qt == 0):
                                p.wait("tensor", "h_qe", 4 * hc + 4)
                            p.wait("vector", "h_bias", nbias)
                            if it >= 2:
                                p.wait("vector", "h_se", it - 1)
                            for ksb in range(4):
                                n_sc += 1
                                psc = ps_sc[n_sc % 2]
                                if n_sc >= 3:
                                    p.wait("tensor", "h_vb", n_sc - 2)
                                hts, _ = p.inc("h_ts", 1)

                                def t_s(te, h2=h2, q0=q0, k0=k0, ksb=ksb, psc=psc, hts=hts):
                                    te.matmul(psc[:, :],
                                              qh[64 * h2:64 * (h2 + 1), q0:q0 + 128],
                                              kh[64 * h2:64 * (h2 + 1), k0 + 512 * ksb:k0 + 512 * (ksb + 1)],
                                              start=True, stop=True).then_inc(hts, 1)
                                p.emit("tensor", t_s)
                                p.wait("vector", "h_ts", n_sc)
                                hvb, _ = p.inc("h_vb", 1)
                                last = ksb == 3

                                def v_b(v, qt=qt, ksb=ksb, psc=psc, hvb=hvb, bbuf=bbuf):
                                    v.scalar_tensor_tensor(
                                        bbuf[:, 512 * ksb:512 * (ksb + 1)], psc[:, :], QSCALE,
                                        bias[:, qt, 512 * ksb:512 * (ksb + 1)], OP.mult, OP.add
                                    ).then_inc(hvb, 1)
                                p.emit("vector", v_b)
                            p.wait("vector", "h_vb", n_sc)
                            if it >= 2:
                                p.wait("vector", "h_se", it - 1)
                            hvm, _ = p.inc("h_vm", 1)
                            p.emit("vector", lambda v, bbuf=bbuf, it=it, hvm=hvm: v.tensor_reduce(
                                rmax[it % 2][:, :], bbuf[:, :], AX.X, OP.max,
                                negate=True).then_inc(hvm, 1))
                            p.wait("scalar", "h_vm", n_it)
                            if it >= 3:
                                p.wait("scalar", "h_tt", 2 * (it - 2))
                            if it >= 3:
                                p.wait("scalar", "h_vr", it - 2)
                            hse, _ = p.inc("h_se", 1)
                            p.emit("scalar", lambda s, bbuf=bbuf, pbuf=pbuf, it=it, hse=hse:
                                   s.activation(pbuf[:, :], bbuf[:, :], AF.Exp,
                                                bias=rmax[it % 2][:, :],
                                                accum_out=rsum[it % 3][:, :]).then_inc(hse, 1))
                            p.defer_start()
                            p.wait("vector", "h_se", n_it)
                            hvr, _ = p.inc("h_vr", 1)

                            def v_norm(v, pbuf=pbuf, it=it, hvr=hvr):
                                p.rt_chain(v, v.reciprocal(rrec[it % 3][:, :], rsum[it % 3][:, :]))
                                v.tensor_scalar(pbuf[:, :], pbuf[:, :], rrec[it % 3][:, :], None,
                                                OP.mult).then_inc(hvr, 1)
                            p.emit("vector", v_norm)
                            p.wait("tensor", "h_vr", n_it)
                            for g in range(2):
                                n_tt += 1
                                if n_tt >= 3:
                                    p.wait("tensor", "h_ae", n_tt - 2)
                                htt, _ = p.inc("h_tt", 1)

                                def t_at(te, pbuf=pbuf, g=g, htt=htt):
                                    pa = ps_at[g]
                                    for k in range(8):
                                        kt = 8 * g + k
                                        inst = te.transpose(pa[:, k, :],
                                                            pbuf[:, 128 * kt:128 * (kt + 1)],
                                                            t["ident"][:, :])
                                    inst.then_inc(htt, 1)
                                p.emit("tensor", t_at)
                                p.wait("vector", "h_tt", n_tt)
                                if n_tt >= 3:
                                    p.wait("vector", "h_tav", it)
                                hae, _ = p.inc("h_ae", 1)
                                p.emit("vector", lambda v, g=g, hae=hae: v.tensor_copy(
                                    at_sb[:, 8 * g:8 * (g + 1), :], ps_at[g][:, :, :]).then_inc(hae, 1))
                            p.wait("tensor", "h_ae", n_tt)
                            if it >= 1:
                                p.wait("tensor", "h_vy", it)
                            hav, _ = p.inc("h_tav", 1)

                            def t_av(te, h2=h2, r4=r4, hav=hav):
                                for kt in range(16):
                                    inst = te.matmul(ps_av[0:64, :],
                                                     vt[:, 16 * r4 + kt, 64 * h2:64 * (h2 + 1)],
                                                     at_sb[:, kt, :],
                                                     start=(kt == 0), stop=(kt == 15))
                                inst.then_inc(hav, 1)
                            p.emit("tensor", t_av)
                            p.wait("vector", "h_tav", n_it)
                            hvy, _ = p.inc("h_vy", 1)
                            p.emit("vector", lambda v, h2=h2, hc=hc, q0=q0, hvy=hvy: v.tensor_copy(
                                t["yfm"][64 * h2:64 * (h2 + 1), hc, q0:q0 + 128],
                                ps_av[0:64, :]).then_inc(hvy, 1))
                            pending.append(p.defer_end())
                            if len(pending) > 2:
                                p.emit_deferred(pending.pop(0))
        flush()


def emit_ffn_chunked(p, ctx, nc, t, xn_fm, acc, wg_dram, wu_dram, wd_dram,
                     n_ic, isz, silu_scale, wacc_scalar, rw, tag, zwait=None):
    nsub = isz // 128
    with ExitStack() as fctx:
        wg_b = [sb(fctx, nc, f"{tag}_wg{i}", [128, DC, isz], BF16) for i in range(2)]
        wu_b = [sb(fctx, nc, f"{tag}_wu{i}", [128, DC, isz], BF16) for i in range(2)]
        wd_b = [sb(fctx, nc, f"{tag}_wd{i}", [128, nsub, 1024], BF16) for i in range(2)]
        hb = sb(fctx, nc, f"{tag}_h", [128, nsub, TOK], BF16)
        sg = [sb(fctx, nc, f"{tag}_sg{i}", [128, 512], BF16) for i in range(2)]
        pg = [ps(fctx, nc, f"{tag}_pg{i}", [128, 512], F32) for i in range(2)]
        pu = [ps(fctx, nc, f"{tag}_pu{i}", [128, 512], F32) for i in range(2)]
        pd = [ps(fctx, nc, f"{tag}_pd{i}", [128, 512], F32) for i in range(2)]

        NT = TOK // 512
        ig = 0
        jd = 0
        for ic in range(n_ic):
            bi = ic % 2
            if ic >= 2:
                p.wait("gpsimd", f"{tag}_tg", nsub * NT * (ic - 1))
                p.wait("gpsimd", f"{tag}_tu", nsub * NT * (ic - 1))
                p.wait("gpsimd", f"{tag}_td", 2 * 16 * (ic - 1))
            hw, _ = p.inc(f"{tag}_dw{bi}", 48)

            def g_w(g, ic=ic, bi=bi, hw=hw):
                g.dma_start(out=wg_b[bi][:, :, :],
                            in_=wg_dram.ap()[:, isz * ic:isz * (ic + 1)]
                            .rearrange("(c p) f -> p c f", p=128)).then_inc(hw, 16)
                g.dma_start(out=wu_b[bi][:, :, :],
                            in_=wu_dram.ap()[:, isz * ic:isz * (ic + 1)]
                            .rearrange("(c p) f -> p c f", p=128)).then_inc(hw, 16)
                g.dma_start(out=wd_b[bi][:, :, :],
                            in_=wd_dram.ap()[isz * ic:isz * (ic + 1), :]
                            .rearrange("(c p) f -> p c f", p=128)).then_inc(hw, 16)
            p.emit("gpsimd", g_w)
            p.wait("tensor", f"{tag}_dw{bi}")
            if ic >= 1:
                p.wait("vector", f"{tag}_td", 2 * 16 * ic, )
            for mc in range(nsub):
                for nc4 in range(NT):
                    ig += 1
                    pgi = pg[ig % 2]
                    pui = pu[ig % 2]
                    if ig >= 3:
                        p.wait("tensor", f"{tag}_sg", ig - 2)
                    htg, _ = p.inc(f"{tag}_tg", 1)

                    def t_g(te, bi=bi, mc=mc, nc4=nc4, pgi=pgi, htg=htg):
                        for kc in range(DC):
                            inst = te.matmul(pgi[:, :], wg_b[bi][:, kc, 128 * mc:128 * (mc + 1)],
                                             xn_fm[:, kc, 512 * nc4:512 * (nc4 + 1)],
                                             start=(kc == 0), stop=(kc == DC - 1))
                        inst.then_inc(htg, 1)
                    p.emit("tensor", t_g)
                    if ig >= 3:
                        p.wait("tensor", f"{tag}_h", ig - 2)
                    htu, _ = p.inc(f"{tag}_tu", 1)

                    def t_u(te, bi=bi, mc=mc, nc4=nc4, pui=pui, htu=htu):
                        for kc in range(DC):
                            inst = te.matmul(pui[:, :], wu_b[bi][:, kc, 128 * mc:128 * (mc + 1)],
                                             xn_fm[:, kc, 512 * nc4:512 * (nc4 + 1)],
                                             start=(kc == 0), stop=(kc == DC - 1))
                        inst.then_inc(htu, 1)
                    p.emit("tensor", t_u)
                    p.wait("scalar", f"{tag}_tg", ig)
                    if ig >= 3:
                        p.wait("scalar", f"{tag}_h", ig - 2)
                    hsg, _ = p.inc(f"{tag}_sg", 1)
                    p.emit("scalar", lambda s, pgi=pgi, ig=ig, hsg=hsg: s.activation(
                        sg[ig % 2][:, :], pgi[:, :], AF.Silu,
                        scale=silu_scale).then_inc(hsg, 1))
                    p.wait("vector", f"{tag}_tu", ig)
                    p.wait("vector", f"{tag}_sg", ig)
                    hh, _ = p.inc(f"{tag}_h", 1)
                    p.emit("vector", lambda v, bi=bi, mc=mc, nc4=nc4, pui=pui, ig=ig, hh=hh:
                           v.tensor_tensor(hb[:, mc, 512 * nc4:512 * (nc4 + 1)],
                                           pui[:, :], sg[ig % 2][:, :], OP.mult).then_inc(hh, 1))
            p.wait("tensor", f"{tag}_h", ig)
            for mt in range(16):
                for nn in range(2):
                    jd += 1
                    pdi = pd[jd % 2]
                    if jd >= 3:
                        p.wait("tensor", f"{tag}_wa", jd - 2)
                    htd, _ = p.inc(f"{tag}_td", 1)

                    def t_d(te, bi=bi, mt=mt, nn=nn, pdi=pdi, htd=htd):
                        for kc in range(nsub):
                            inst = te.matmul(pdi[:, :], hb[:, kc, 128 * mt:128 * (mt + 1)],
                                             wd_b[bi][:, kc, 512 * nn:512 * (nn + 1)],
                                             start=(kc == 0), stop=(kc == nsub - 1))
                        inst.then_inc(htd, 1)
                    p.emit("tensor", t_d)
                    p.wait("vector", f"{tag}_td", jd)
                    if zwait and jd == 1:
                        p.wait("vector", zwait, 1)
                    hwa, _ = p.inc(f"{tag}_wa", 1)
                    scal = wacc_scalar if rw is None else rw(mt)
                    p.emit("vector", lambda v, mt=mt, nn=nn, pdi=pdi, scal=scal, hwa=hwa:
                           v.scalar_tensor_tensor(acc[:, mt, 512 * nn:512 * (nn + 1)],
                                                  pdi[:, :], scal,
                                                  acc[:, mt, 512 * nn:512 * (nn + 1)],
                                                  OP.mult, OP.add).then_inc(hwa, 1))
        p.wait("vector", f"{tag}_wa", jd)




def emit_routing(p, ctx, nc, t, xn_fm, gate_wT_sb, bias_bc, rw):
    with ExitStack() as rctx:
        ps_g = [ps(rctx, nc, f"r_ps{i}", [128, 64], F32) for i in range(2)]
        sco = [sb(rctx, nc, f"r_sco{i}", [128, 64], F32) for i in range(2)]
        sfc = sb(rctx, nc, "r_sfc", [128, 64], F32)
        grp2 = sb(rctx, nc, "r_grp2", [128, 64], F32)
        m1 = sb(rctx, nc, "r_m1", [128, 8], F32)
        m2 = sb(rctx, nc, "r_m2", [128, 8], F32)
        gs = sb(rctx, nc, "r_gs", [128, 8], F32)
        gm = sb(rctx, nc, "r_gm", [128, 1], F32)
        gsel = sb(rctx, nc, "r_gsel", [128, 8], F32)
        gmask = sb(rctx, nc, "r_gmask", [128, 8], F32)
        wm = sb(rctx, nc, "r_wm", [128, 64], F32)
        msk = sb(rctx, nc, "r_msk", [128, 64], F32)
        mm = sb(rctx, nc, "r_mm", [128, 1], F32)
        sel = sb(rctx, nc, "r_sel", [128, 64], F32)
        den = sb(rctx, nc, "r_den", [128, 1], F32)
        for pt in range(16):
            if pt >= 2:
                p.wait("tensor", "r_sig", pt - 1)
            htr, _ = p.inc("r_tg", 1)

            def t_r(te, pt=pt, htr=htr):
                for kc in range(DC):
                    inst = te.matmul(ps_g[pt % 2][:, :], xn_fm[:, kc, 128 * pt:128 * (pt + 1)],
                                     gate_wT_sb[:, kc, :], start=(kc == 0), stop=(kc == DC - 1))
                inst.then_inc(htr, 1)
            p.emit("tensor", t_r)
            p.wait("scalar", "r_tg", pt + 1)
            if pt >= 2:
                p.wait("scalar", "r_rw", pt - 1)
            hsg, _ = p.inc("r_sig", 1)
            p.emit("scalar", lambda s, pt=pt, hsg=hsg: s.activation(
                sco[pt % 2][:, :], ps_g[pt % 2][:, :], AF.Sigmoid).then_inc(hsg, 1))
            p.wait("vector", "r_sig", pt + 1)
            hrw, _ = p.inc("r_rw", 1)

            def v_route(v, pt=pt, hrw=hrw):
                C = lambda inst: p.rt_chain(v, inst)
                g3 = sfc[:, :].rearrange("p (g i) -> p g i", i=8)
                C(v.tensor_tensor(sfc[:, :], sco[pt % 2][:, :], bias_bc[:, :], OP.add))
                C(v.tensor_reduce(m1[:, :], g3, AX.X, OP.max))
                C(v.tensor_tensor(grp2[:, :].rearrange("p (g i) -> p g i", i=8), g3,
                                  m1[:, :].unsqueeze(2).broadcast_to((128, 8, 8)), OP.is_equal))
                C(v.scalar_tensor_tensor(grp2[:, :], grp2[:, :], NEG, sfc[:, :], OP.mult, OP.add))
                C(v.tensor_reduce(m2[:, :], grp2[:, :].rearrange("p (g i) -> p g i", i=8), AX.X, OP.max))
                C(v.tensor_tensor(gs[:, :], m1[:, :], m2[:, :], OP.add))
                C(v.memset(gmask[:, :], 0.0))
                for it in range(4):
                    C(v.tensor_reduce(gm[:, :], gs[:, :], AX.X, OP.max))
                    C(v.tensor_scalar(gsel[:, :], gs[:, :], gm[:, :], None, OP.is_equal))
                    C(v.tensor_tensor(gmask[:, :], gmask[:, :], gsel[:, :], OP.add))
                    C(v.scalar_tensor_tensor(gs[:, :], gsel[:, :], NEG, gs[:, :], OP.mult, OP.add))
                C(v.tensor_tensor(msk[:, :].rearrange("p (g i) -> p g i", i=8),
                                  sfc[:, :].rearrange("p (g i) -> p g i", i=8),
                                  gmask[:, :].unsqueeze(2).broadcast_to((128, 8, 8)), OP.mult))
                C(v.memset(wm[:, :], 0.0))
                for it in range(8):
                    C(v.tensor_reduce(mm[:, :], msk[:, :], AX.X, OP.max))
                    C(v.tensor_scalar(sel[:, :], msk[:, :], mm[:, :], None, OP.is_equal))
                    C(v.tensor_tensor(wm[:, :], wm[:, :], sel[:, :], OP.add))
                    C(v.scalar_tensor_tensor(msk[:, :], sel[:, :], NEG, msk[:, :], OP.mult, OP.add))
                C(v.tensor_tensor(wm[:, :], wm[:, :], sco[pt % 2][:, :], OP.mult))
                C(v.tensor_reduce(den[:, :], wm[:, :], AX.X, OP.add))
                C(v.tensor_scalar(den[:, :], den[:, :], 1e-20, None, OP.add))
                C(v.reciprocal(den[:, :], den[:, :]))
                C(v.tensor_scalar(den[:, :], den[:, :], 2.5, None, OP.mult))
                v.tensor_scalar(rw[:, pt, :], wm[:, :], den[:, :], None, OP.mult).then_inc(hrw, 1)
            p.emit("vector", v_route)
        p.wait("vector", "r_rw", 16)


def emit_moe(p, ctx, nc, xn_fm, acc, experts, tag, zwait=None):
    with ExitStack() as fctx:
        imax = max(sum(e["isub"]) for e in experts)
        nsmax = max(len(e["isub"]) for e in experts)
        wg_b = [sb(fctx, nc, f"{tag}_wg{i}", [128, DC, imax], BF16) for i in range(2)]
        wu_b = [sb(fctx, nc, f"{tag}_wu{i}", [128, DC, imax], BF16) for i in range(2)]
        wd_b = [sb(fctx, nc, f"{tag}_wd{i}", [128, nsmax, 1024], BF16) for i in range(2)]
        hb = sb(fctx, nc, f"{tag}_h", [128, nsmax, TOK], BF16)
        sg = [sb(fctx, nc, f"{tag}_sg{i}", [128, 512], BF16) for i in range(2)]
        pg = [ps(fctx, nc, f"{tag}_pg{i}", [128, 512], F32) for i in range(2)]
        pu = [ps(fctx, nc, f"{tag}_pu{i}", [128, 512], F32) for i in range(2)]
        pd = [ps(fctx, nc, f"{tag}_pd{i}", [128, 512], F32) for i in range(2)]
        NT = TOK // 512
        ig = 0
        jd = 0
        n_tgu = [0, 0]
        n_td = [0, 0]
        for ic, ex in enumerate(experts):
            bi = ic % 2
            isub = ex["isub"]
            nsub = len(isub)
            ioffs = [sum(isub[:k]) for k in range(nsub)]
            I = sum(isub)
            if ic >= 2:
                p.wait("gpsimd", f"{tag}_tg", n_tgu[bi])
                p.wait("gpsimd", f"{tag}_tu", n_tgu[bi])
                p.wait("gpsimd", f"{tag}_td", n_td[bi])
            ndma = 48 if ex.get("wd_tail") is None else 64
            hw, _ = p.inc(f"{tag}_dw{bi}", ndma)

            def g_w(g, ex=ex, bi=bi, hw=hw, I=I, nsub=nsub):
                g.dma_start(out=wg_b[bi][:, :, 0:I],
                            in_=ex["wg"].rearrange("(c p) f -> p c f", p=128)).then_inc(hw, 16)
                g.dma_start(out=wu_b[bi][:, :, 0:I],
                            in_=ex["wu"].rearrange("(c p) f -> p c f", p=128)).then_inc(hw, 16)
                nmain = ex["wd_main"].shape[0] // 128
                g.dma_start(out=wd_b[bi][:, 0:nmain, :],
                            in_=ex["wd_main"].rearrange("(c p) f -> p c f", p=128)).then_inc(hw, 16)
                if ex.get("wd_tail") is not None:
                    g.dma_start(out=wd_b[bi][0:ex["wd_tail"].shape[0], nsub - 1, :],
                                in_=ex["wd_tail"]).then_inc(hw, 16)
            p.emit("gpsimd", g_w)
            p.wait("tensor", f"{tag}_dw{bi}")
            if ic >= 1:
                p.wait("vector", f"{tag}_td", jd)
            n_tgu[bi] = 0
            for mc in range(nsub):
                sz = isub[mc]
                io = ioffs[mc]
                for nc4 in range(NT):
                    ig += 1
                    pgi = pg[ig % 2]
                    pui = pu[ig % 2]
                    if ig >= 3:
                        p.wait("tensor", f"{tag}_sg", ig - 2)
                    htg, _ = p.inc(f"{tag}_tg", 1)

                    def t_g(te, bi=bi, io=io, sz=sz, nc4=nc4, pgi=pgi, htg=htg):
                        for kc in range(DC):
                            inst = te.matmul(pgi[0:sz, :], wg_b[bi][:, kc, io:io + sz],
                                             xn_fm[:, kc, 512 * nc4:512 * (nc4 + 1)],
                                             start=(kc == 0), stop=(kc == DC - 1))
                        inst.then_inc(htg, 1)
                    p.emit("tensor", t_g)
                    if ig >= 3:
                        p.wait("tensor", f"{tag}_h", ig - 2)
                    htu, _ = p.inc(f"{tag}_tu", 1)

                    def t_u(te, bi=bi, io=io, sz=sz, nc4=nc4, pui=pui, htu=htu):
                        for kc in range(DC):
                            inst = te.matmul(pui[0:sz, :], wu_b[bi][:, kc, io:io + sz],
                                             xn_fm[:, kc, 512 * nc4:512 * (nc4 + 1)],
                                             start=(kc == 0), stop=(kc == DC - 1))
                        inst.then_inc(htu, 1)
                    p.emit("tensor", t_u)
                    p.wait("scalar", f"{tag}_tg", ig)
                    if ig >= 3:
                        p.wait("scalar", f"{tag}_h", ig - 2)
                    hsg, _ = p.inc(f"{tag}_sg", 1)
                    p.emit("scalar", lambda s, pgi=pgi, sz=sz, ig=ig, hsg=hsg, sc=ex["scale"]:
                           s.activation(sg[ig % 2][0:sz, :], pgi[0:sz, :], AF.Silu,
                                        scale=sc).then_inc(hsg, 1))
                    p.wait("vector", f"{tag}_tu", ig)
                    p.wait("vector", f"{tag}_sg", ig)
                    hh, _ = p.inc(f"{tag}_h", 1)
                    p.emit("vector", lambda v, mc=mc, sz=sz, nc4=nc4, pui=pui, ig=ig, hh=hh:
                           v.tensor_tensor(hb[0:sz, mc, 512 * nc4:512 * (nc4 + 1)],
                                           pui[0:sz, :], sg[ig % 2][0:sz, :], OP.mult).then_inc(hh, 1))
                    n_tgu[bi] += 0
            n_tgu[bi] = p.rel(f"{tag}_tg")
            p.wait("tensor", f"{tag}_h", ig)
            for mt in range(16):
                for nn in range(2):
                    jd += 1
                    pdi = pd[jd % 2]
                    if jd >= 3:
                        p.wait("tensor", f"{tag}_wa", jd - 2)
                    htd, _ = p.inc(f"{tag}_td", 1)

                    def t_d(te, bi=bi, mt=mt, nn=nn, pdi=pdi, htd=htd, isub=isub, ioffs=ioffs):
                        for kc in range(len(isub)):
                            inst = te.matmul(pdi[:, :], hb[0:isub[kc], kc, 128 * mt:128 * (mt + 1)],
                                             wd_b[bi][0:isub[kc], kc, 512 * nn:512 * (nn + 1)],
                                             start=(kc == 0), stop=(kc == len(isub) - 1))
                        inst.then_inc(htd, 1)
                    p.emit("tensor", t_d)
                    p.wait("vector", f"{tag}_td", jd)
                    if zwait and jd == 1:
                        p.wait("vector", zwait, 1)
                    hwa, _ = p.inc(f"{tag}_wa", 1)
                    scal = ex["wscal"] if ex.get("rw") is None else ex["rw"](mt)
                    p.emit("vector", lambda v, mt=mt, nn=nn, pdi=pdi, scal=scal, hwa=hwa:
                           v.scalar_tensor_tensor(acc[:, mt, 512 * nn:512 * (nn + 1)],
                                                  pdi[:, :], scal,
                                                  acc[:, mt, 512 * nn:512 * (nn + 1)],
                                                  OP.mult, OP.add).then_inc(hwa, 1))
            n_td[bi] = p.rel(f"{tag}_td")
        p.wait("vector", f"{tag}_wa", jd)


def _l1_drams(nc, x1_kind="ExternalInput"):
    dd = {}
    if x1_kind is not None:
        dd["x1"] = nc.dram_tensor("x1_own", [TOK, D], F32, kind=x1_kind)
    dd["wq"] = nc.dram_tensor("wqT", [D, D], F32, kind="ExternalInput")
    dd["wk"] = nc.dram_tensor("wkT", [D, D], F32, kind="ExternalInput")
    dd["wv"] = nc.dram_tensor("wvT", [D, D], F32, kind="ExternalInput")
    dd["wo"] = nc.dram_tensor("woT", [D, D], F32, kind="ExternalInput")
    dd["bo"] = nc.dram_tensor("b_o", [1, D], F32, kind="ExternalInput")
    dd["rk"] = nc.dram_tensor("rk_t", [B, B, D], F32, kind="ExternalInput")
    dd["rv"] = nc.dram_tensor("rv_t", [B, B, D], F32, kind="ExternalInput")
    dd["gw"] = nc.dram_tensor("gate_wT", [D, 64], F32, kind="ExternalInput")
    dd["gb"] = nc.dram_tensor("gate_bias", [1, 64], F32, kind="ExternalInput")
    dd["Wg"] = nc.dram_tensor("WgT", [MOE_E, D, MOE_I], F32, kind="ExternalInput")
    dd["Wu"] = nc.dram_tensor("WuT", [MOE_E, D, MOE_I], F32, kind="ExternalInput")
    dd["Wd"] = nc.dram_tensor("WdT", [MOE_E, MOE_I, D], F32, kind="ExternalInput")
    dd["Sg"] = nc.dram_tensor("SgT", [D, S_I], F32, kind="ExternalInput")
    dd["Su"] = nc.dram_tensor("SuT", [D, S_I], F32, kind="ExternalInput")
    dd["Sd"] = nc.dram_tensor("SdT", [S_I, D], F32, kind="ExternalInput")
    dd["y"] = nc.dram_tensor("y_own", [TOK, D], F32, kind="ExternalOutput")
    dd["s1"] = nc.dram_tensor("s1_park", [TOK, D], BF16, kind="Internal")
    return dd


def build_l1(upto="full", n_experts=MOE_E):
    nc = bass.Bass("TRN2", target_bir_lowering=False)
    ctx = ExitStack()
    nc._ctx_keepalive = ctx
    dd = _l1_drams(nc)
    p = Prog(nc, ctx)
    t = {}
    t["ident"] = sb(ctx, nc, "ident", [128, 128], BF16)
    t["ones1"] = sb(ctx, nc, "ones1", [1, 128], BF16)
    t["ones1f"] = sb(ctx, nc, "ones1f", [1, 128], F32)
    with nc.sbuf_tensor("dmp1", [128, 128], F32) as dmp1:
        hg, _ = p.inc("g_ini", 1)

        def g_init(g, hg=hg):
            g.iota(dmp1[:, :], [[1, 128]], base=0, channel_multiplier=-1,
                   allow_small_or_imprecise_dtypes=True)
            g.memset(t["ones1"][:, :], 1.0)
            g.memset(t["ones1f"][:, :], 1.0).then_inc(hg, 1)
        p.emit("gpsimd", g_init)
        p.wait("vector", "g_ini")
        hv, _ = p.inc("v_ini", 1)
        p.emit("vector", lambda v, hv=hv: v.tensor_scalar(
            t["ident"][:, :], dmp1[:, :], 0.0, None, OP.is_equal).then_inc(hv, 1))
        emit_l1_body(p, ctx, nc, t, dd, upto, n_experts)
    with nc.Block() as block:
        p.build(block)
    return nc


def emit_l1_body(p, ctx, nc, t, dd, upto="full", n_experts=MOE_E):
    d_x1 = dd["x1"]
    d_wq, d_wk, d_wv, d_wo, d_bo = dd["wq"], dd["wk"], dd["wv"], dd["wo"], dd["bo"]
    d_rk, d_rv, d_gw, d_gb = dd["rk"], dd["rv"], dd["gw"], dd["gb"]
    d_Wg, d_Wu, d_Wd = dd["Wg"], dd["Wu"], dd["Wd"]
    d_Sg, d_Su, d_Sd = dd["Sg"], dd["Su"], dd["Sd"]
    d_y, d_s1 = dd["y"], dd["s1"]
    t["xnf1"] = sb(ctx, nc, "xn1f", [128, DC, TOK], BF16, side="right")
    x1_r = d_x1.ap().rearrange("(c p) f -> p c f", p=128)

    if True:
        with ExitStack() as bctx:
            bxin = [sb(bctx, nc, f"l1xin{i}", [128, 1024], F32) for i in range(2)]
            bxnt = [sb(bctx, nc, f"l1xnt{i}", [128, 1024], BF16) for i in range(2)]
            lns = make_ln_scratch(bctx, nc, "lnL1")
            bps_x = [ps(bctx, nc, f"l1psx{i}", [128, DC, 128], BF16) for i in range(2)]
            for pt in range(16):
                bi = pt % 2
                if pt >= 2:
                    p.wait("sync", "l_ln", pt - 1)
                hx, _ = p.inc(f"l_dx{bi}", 16)
                p.emit("sync", lambda s, pt=pt, bi=bi, hx=hx: s.dma_start(
                    out=bxin[bi][:, :], in_=x1_r[:, pt, :]).then_inc(hx, 16))
                if pt >= 2:
                    p.wait("vector", "l_tr", pt - 1)
                emit_ln_tile(p, bxin[bi][:, :], bxnt[bi][:, :], lns, "l_ln",
                             wait_before=(f"l_dx{bi}", None))
                p.wait("tensor", "l_ln", pt + 1)
                p.wait("tensor", "v_ini", 1)
                if pt >= 2:
                    p.wait("tensor", "l_xf", pt - 1)
                htr, _ = p.inc("l_tr", 1)

                def t_tr(te, bi=bi, htr=htr):
                    for k in range(DC):
                        inst = te.transpose(bps_x[bi][:, k, :], bxnt[bi][:, 128 * k:128 * (k + 1)],
                                            t["ident"][:, :])
                    inst.then_inc(htr, 1)
                p.emit("tensor", t_tr)
                p.wait("vector", "l_tr", pt + 1)
                hxf, _ = p.inc("l_xf", 1)
                p.emit("vector", lambda v, bi=bi, pt=pt, hxf=hxf: v.tensor_copy(
                    t["xnf1"][:, :, 128 * pt:128 * (pt + 1)], bps_x[bi][:, :, :]).then_inc(hxf, 1))
            p.barrier()

    if upto == "ln":
        _emit_dummy(p, nc, d_y)
        return True

    emit_l1_attention(p, ctx, nc, t, d_wq, d_wk, d_wv, d_wo, d_bo, d_rk, d_rv, d_s1)
    p.barrier()
    if upto == "attn":
        _emit_dummy(p, nc, d_y)
        return True

    t["acc1"] = sb(ctx, nc, "acc1", [128, 16, 1024], F32, side="right")
    t["rw"] = sb(ctx, nc, "rw", [128, 16, 64], F32, side="right")
    hz, _ = p.inc("m_z", 1)
    p.emit("vector", lambda v, hz=hz: v.memset(t["acc1"][:, :, :], 0.0).then_inc(hz, 1))
    with ExitStack() as rctx:
        gwsb = sb(rctx, nc, "gwsb", [128, DC, 64], BF16)
        gbias = sb(rctx, nc, "gbias", [128, 64], F32)
        gbrow = sb(rctx, nc, "gbrow", [1, 64], F32)
        ps_b = ps(rctx, nc, "ps_gb", [128, 64], F32)
        hwg, _ = p.inc("m_dgw", 32)

        def g_gw(g, hwg=hwg):
            g.dma_start(out=gwsb[:, :, :],
                        in_=d_gw.ap().rearrange("(c p) f -> p c f", p=128)).then_inc(hwg, 16)
            g.dma_start(out=gbrow[:, :], in_=d_gb.ap()).then_inc(hwg, 16)
        p.emit("gpsimd", g_gw)
        p.wait("tensor", "m_dgw", 32)
        htb, _ = p.inc("m_tb", 1)
        p.emit("tensor", lambda te, htb=htb: te.matmul(
            ps_b[:, :], t["ones1f"][:, :], gbrow[:, :], start=True, stop=True).then_inc(htb, 1))
        p.wait("vector", "m_tb", 1)
        hvb, _ = p.inc("m_vb", 1)
        p.emit("vector", lambda v, hvb=hvb: v.tensor_copy(gbias[:, :], ps_b[:, :]).then_inc(hvb, 1))
        p.wait("vector", "m_vb", 1)
        emit_routing(p, rctx, nc, t, t["xnf1"], gwsb, gbias, t["rw"])
    p.barrier()
    if upto == "route":
        _emit_dummy(p, nc, d_y)
        return True

    experts = []
    for e in range(n_experts):
        experts.append({
            "wg": d_Wg.ap()[e], "wu": d_Wu.ap()[e],
            "wd_main": d_Wd.ap()[e][0:640, :], "wd_tail": d_Wd.ap()[e][640:704, :],
            "isub": [128] * 5 + [64], "scale": 1.0,
            "rw": (lambda e: (lambda mt: t["rw"][:, mt, e:e + 1]))(e),
        })
    for sh in range(2):
        experts.append({
            "wg": d_Sg.ap()[:, 704 * sh:704 * (sh + 1)],
            "wu": d_Su.ap()[:, 704 * sh:704 * (sh + 1)],
            "wd_main": d_Sd.ap()[704 * sh:704 * sh + 640, :],
            "wd_tail": d_Sd.ap()[704 * sh + 640:704 * (sh + 1), :],
            "isub": [128] * 5 + [64], "scale": 1.0, "rw": None, "wscal": 1.0,
        })
    emit_moe(p, ctx, nc, t["xnf1"], t["acc1"], experts, "m", zwait="m_z")
    p.barrier()
    if upto == "moe":
        _emit_dummy(p, nc, d_y)
        return True

    y_r = d_y.ap().rearrange("(c p) f -> p c f", p=128)
    s1_r = d_s1.ap().rearrange("(c p) f -> p c f", p=128)
    with ExitStack() as ectx:
        gbuf = [sb(ectx, nc, f"f1g{i}", [128, 1024], BF16) for i in range(2)]
        xo2 = [sb(ectx, nc, f"f1x{i}", [128, 1024], F32) for i in range(2)]
        s1b = [sb(ectx, nc, f"f1sb{i}", [128, 1024], BF16) for i in range(2)]
        tbuf = [sb(ectx, nc, f"f1t{i}", [128, 1024], F32) for i in range(2)]
        for pt in range(16):
            bi = pt % 2
            if pt >= 2:
                p.wait("sync", "f1v", pt - 1)
            hx, _ = p.inc(f"f1dx{bi}", 16)

            def s_ld(s, pt=pt, bi=bi, hx=hx):
                s.dma_start(out=xo2[bi][:, :], in_=x1_r[:, pt, :]).then_inc(hx, 8 if False else 16)
            p.emit("sync", s_ld)
            hx2, _ = p.inc(f"f1ds{bi}", 16)
            p.emit("sync", lambda s, pt=pt, bi=bi, hx2=hx2: s.dma_start(
                out=s1b[bi][:, :], in_=s1_r[:, pt, :]).then_inc(hx2, 16))
            if pt >= 2:
                p.wait("scalar", "f1v", pt - 1)
            hg2, _ = p.inc("f1s", 1)
            p.emit("scalar", lambda s, pt=pt, bi=bi, hg2=hg2: s.activation(
                gbuf[bi][:, :], t["acc1"][:, pt, :], AF.Silu).then_inc(hg2, 1))
            p.wait("vector", "f1s", pt + 1)
            p.wait("vector", f"f1dx{bi}")
            p.wait("vector", f"f1ds{bi}")
            if pt >= 2:
                p.wait("vector", f"f1dw{pt % 2}")
            hv2, _ = p.inc("f1v", 1)

            def v_comb(v, pt=pt, bi=bi, hv2=hv2):
                p.rt_chain(v, v.tensor_tensor(
                    tbuf[bi][:, :], s1b[bi][:, :], gbuf[bi][:, :], OP.mult))
                v.tensor_tensor(tbuf[bi][:, :], xo2[bi][:, :], tbuf[bi][:, :], OP.add).then_inc(hv2, 1)
            p.emit("vector", v_comb)
            p.wait("sync", "f1v", pt + 1)
            hw2, _ = p.inc(f"f1dw{pt % 2}", 16)
            p.emit("sync", lambda s, pt=pt, bi=bi, hw2=hw2: s.dma_start(
                out=y_r[:, pt, :], in_=tbuf[bi][:, :]).then_inc(hw2, 16))
        p.wait("sync", "f1dw0")
        p.wait("sync", "f1dw1")
    return False


def emit_l1_attention(p, ctx, nc, t, d_wq, d_wk, d_wv, d_wo, d_bo, d_rk, d_rv, d_s1):
    with ExitStack() as actx:
        y1 = sb(actx, nc, "y1tm", [128, 16, 1024], BF16)
        with ExitStack() as qctx:
            qtm = sb(qctx, nc, "qtm", [128, 16, 1024], BF16)
            ktm = sb(qctx, nc, "ktm", [128, 16, 1024], BF16)
            vtm = sb(qctx, nc, "vtm", [128, 16, 1024], BF16)
            with ExitStack() as wctx:
                wsb = sb(wctx, nc, "l1w", [128, DC, 1024], BF16)
                ps_p = [ps(wctx, nc, f"l1psp{i}", [128, 512], F32) for i in range(2)]
                for wi, (dw, dst) in enumerate([(d_wq, qtm), (d_wk, ktm), (d_wv, vtm)]):
                    if wi >= 1:
                        p.wait("gpsimd", "n_tp", 32 * wi)
                    hwd, _ = p.inc(f"n_dw{wi % 2}", 16)
                    p.emit("gpsimd", lambda g, dw=dw, hwd=hwd: g.dma_start(
                        out=wsb[:, :, :],
                        in_=dw.ap().rearrange("(c p) f -> p c f", p=128)).then_inc(hwd, 16))
                    p.wait("tensor", f"n_dw{wi % 2}")
                    for mt in range(16):
                        for nn in range(2):
                            i2 = 32 * wi + mt * 2 + nn
                            pp = ps_p[i2 % 2]
                            if i2 >= 2:
                                p.wait("tensor", "n_pe", i2 - 1)
                            htp, _ = p.inc("n_tp", 1)

                            def t_p(te, mt=mt, nn=nn, pp=pp, htp=htp):
                                for kc in range(DC):
                                    inst = te.matmul(
                                        pp[:, :], t["xnf1"][:, kc, 128 * mt:128 * (mt + 1)],
                                        wsb[:, kc, 512 * nn:512 * (nn + 1)],
                                        start=(kc == 0), stop=(kc == DC - 1))
                                inst.then_inc(htp, 1)
                            p.emit("tensor", t_p)
                            p.wait("scalar", "n_tp", i2 + 1)
                            hpe, _ = p.inc("n_pe", 1)
                            p.emit("scalar", lambda s, mt=mt, nn=nn, pp=pp, dst=dst, hpe=hpe:
                                   s.activation(dst[:, mt, 512 * nn:512 * (nn + 1)],
                                                pp[:, :], AF.Copy).then_inc(hpe, 1))
                p.barrier()
            with ExitStack() as sctx:
                rkb = sb(sctx, nc, "rkb", [128, B, 1024], BF16)
                rvb = sb(sctx, nc, "rvb", [128, B, 1024], BF16)
                kk = sb(sctx, nc, "kk", [128, 1024], BF16)
                mmb = sb(sctx, nc, "mmb", [128, 1024], F32)
                scoresb = sb(sctx, nc, "scoresb", [128, 8, 16], F32)
                nmax = sb(sctx, nc, "nmax", [128, 1], F32)
                pexp = sb(sctx, nc, "pexp", [128, 8, 16], BF16)
                ssum = sb(sctx, nc, "ssumh", [128, 8], F32)
                srec = sb(sctx, nc, "srech", [128, 8], F32)
                yac = sb(sctx, nc, "yac", [128, 1024], F32)
                n_it = 0
                for s_ in range(B):
                    p.wait("gpsimd", "n_vy", n_it)
                    hrk, _ = p.inc(f"n_drk{s_ % 2}", 32)
                    p.emit("gpsimd", lambda sy, s_=s_, hrk=hrk: (
                        sy.dma_start(out=rkb[:, :, :],
                                     in_=d_rk.ap()[s_].unsqueeze(0).broadcast_to((128, B, D))
                                     ).then_inc(hrk, 16),
                        sy.dma_start(out=rvb[:, :, :],
                                     in_=d_rv.ap()[s_].unsqueeze(0).broadcast_to((128, B, D))
                                     ).then_inc(hrk, 16)))
                    for pt2 in range(2):
                        it = n_it
                        n_it += 1
                        pt = 2 * s_ + pt2
                        p.wait("vector", f"n_drk{s_ % 2}")
                        if it == 0:
                            p.wait("vector", "n_pe", 96)
                        hsc, _ = p.inc("n_sc", 1)

                        def v_scores(v, s_=s_, pt2=pt2, hsc=hsc):
                            C = lambda inst: p.rt_chain(v, inst)
                            for tt in range(B):
                                C(v.tensor_tensor(kk[:, :], ktm[:, 2 * tt + pt2, :],
                                                  rkb[:, tt, :], OP.add))
                                C(v.tensor_tensor(mmb[:, :], qtm[:, 2 * s_ + pt2, :],
                                                  kk[:, :], OP.mult))
                                C(v.tensor_reduce(scoresb[:, tt, :],
                                                  mmb[:, :].rearrange("p (h d) -> p h d", d=HDIM),
                                                  AX.X, OP.add))
                            C(v.tensor_scalar(scoresb[:, :, :], scoresb[:, :, :], QSCALE,
                                              None, OP.mult))
                            v.tensor_reduce(nmax[:, :], scoresb[:, :, :], AX.XY, OP.max,
                                            negate=True).then_inc(hsc, 1)
                        p.emit("vector", v_scores)
                        p.wait("scalar", "n_sc", n_it)
                        hex_, _ = p.inc("n_ex", 1)
                        p.emit("scalar", lambda s, hex_=hex_: s.activation(
                            pexp[:, :, :], scoresb[:, :, :], AF.Exp,
                            bias=nmax[:, :]).then_inc(hex_, 1))
                        p.wait("vector", "n_ex", n_it)
                        hvy, _ = p.inc("n_vy", 1)

                        def v_av(v, s_=s_, pt2=pt2, pt=pt, hvy=hvy):
                            C = lambda inst: p.rt_chain(v, inst)
                            C(v.tensor_reduce(ssum[:, :], pexp[:, :, :], AX.X, OP.add))
                            C(v.reciprocal(srec[:, :], ssum[:, :]))
                            C(v.memset(yac[:, :], 0.0))
                            for tt in range(B):
                                C(v.tensor_scalar(pexp[:, tt, :], pexp[:, tt, :],
                                                  srec[:, tt:tt + 1], None, OP.mult))
                                C(v.tensor_tensor(kk[:, :], vtm[:, 2 * tt + pt2, :],
                                                  rvb[:, tt, :], OP.add))
                                C(v.tensor_tensor(
                                    mmb[:, :].rearrange("p (h d) -> p h d", d=HDIM),
                                    pexp[:, tt, :].unsqueeze(2).broadcast_to((128, 16, 64)),
                                    kk[:, :].rearrange("p (h d) -> p h d", d=HDIM), OP.mult))
                                C(v.tensor_tensor(yac[:, :], yac[:, :], mmb[:, :], OP.add))
                            v.tensor_copy(y1[:, pt, :], yac[:, :]).then_inc(hvy, 1)
                        p.emit("vector", v_av)
                p.wait("vector", "n_vy", n_it)
                p.barrier()
        with ExitStack() as octx:
            y1f = sb(octx, nc, "y1f", [128, DC, TOK], BF16)
            wsb2 = sb(octx, nc, "l1wo", [128, DC, 1024], BF16)
            bobc = sb(octx, nc, "bobc", [128, 1024], F32)
            borow = sb(octx, nc, "borow", [1, 1024], F32)
            s1sb = sb(octx, nc, "s1sb", [128, 2, 1024], BF16)
            ps_t = [ps(octx, nc, f"l1pst{i}", [128, DC, 128], BF16) for i in range(2)]
            ps_p = [ps(octx, nc, f"l1psq{i}", [128, 512], F32) for i in range(2)]
            ps_bo = ps(octx, nc, "ps_bo", [128, 1024], F32)
            hwo, _ = p.inc("n_dwo", 32)

            def g_wo(g, hwo=hwo):
                g.dma_start(out=wsb2[:, :, :],
                            in_=d_wo.ap().rearrange("(c p) f -> p c f", p=128)).then_inc(hwo, 16)
                g.dma_start(out=borow[:, :], in_=d_bo.ap()).then_inc(hwo, 16)
            p.emit("gpsimd", g_wo)
            p.wait("tensor", "n_dwo", 32)
            p.wait("tensor", "v_ini", 1)
            hgb, _ = p.inc("n_tbo", 1)

            def t_bo(te, hgb=hgb):
                te.matmul(ps_bo[:, 0:512], t["ones1f"][:, :], borow[:, 0:512],
                          start=True, stop=True)
                te.matmul(ps_bo[:, 512:1024], t["ones1f"][:, :], borow[:, 512:1024],
                          start=True, stop=True).then_inc(hgb, 1)
            p.emit("tensor", t_bo)
            p.wait("vector", "n_tbo", 1)
            hvbo, _ = p.inc("n_vbo", 1)
            p.emit("vector", lambda v, hvbo=hvbo: v.tensor_copy(
                bobc[:, :], ps_bo[:, :]).then_inc(hvbo, 1))
            for pt in range(16):
                bi = pt % 2
                if pt >= 2:
                    p.wait("tensor", "n_yf", pt - 1)
                hyt, _ = p.inc("n_yt", 1)

                def t_ytr(te, pt=pt, bi=bi, hyt=hyt):
                    for k in range(DC):
                        inst = te.transpose(ps_t[bi][:, k, :], y1[:, pt, 128 * k:128 * (k + 1)],
                                            t["ident"][:, :])
                    inst.then_inc(hyt, 1)
                p.emit("tensor", t_ytr)
                p.wait("vector", "n_yt", pt + 1)
                hyf, _ = p.inc("n_yf", 1)
                p.emit("vector", lambda v, pt=pt, bi=bi, hyf=hyf: v.tensor_copy(
                    y1f[:, :, 128 * pt:128 * (pt + 1)], ps_t[bi][:, :, :]).then_inc(hyf, 1))
            s1_r = d_s1.ap().rearrange("(c p) f -> p c f", p=128)
            p.wait("tensor", "n_yf", 16)
            p.wait("vector", "n_vbo", 1)
            for mt in range(16):
                for nn in range(2):
                    i2 = mt * 2 + nn
                    pp = ps_p[i2 % 2]
                    if i2 >= 2:
                        p.wait("tensor", "n_se", i2 - 1)
                    hto, _ = p.inc("n_to", 1)

                    def t_o(te, mt=mt, nn=nn, pp=pp, hto=hto):
                        for kc in range(DC):
                            inst = te.matmul(pp[:, :], y1f[:, kc, 128 * mt:128 * (mt + 1)],
                                             wsb2[:, kc, 512 * nn:512 * (nn + 1)],
                                             start=(kc == 0), stop=(kc == DC - 1))
                        inst.then_inc(hto, 1)
                    p.emit("tensor", t_o)
                    p.wait("vector", "n_to", i2 + 1)
                    if i2 >= 4 and nn == 0:
                        p.wait("vector", f"n_ds{mt % 2}")
                    hse, _ = p.inc("n_se", 1)
                    p.emit("vector", lambda v, mt=mt, nn=nn, pp=pp, hse=hse: v.tensor_tensor(
                        s1sb[:, mt % 2, 512 * nn:512 * (nn + 1)], pp[:, :],
                        bobc[:, 512 * nn:512 * (nn + 1)], OP.add).then_inc(hse, 1))
                p.wait("sync", "n_se", 2 * mt + 2)
                hds, _ = p.inc(f"n_ds{mt % 2}", 16)
                p.emit("sync", lambda sy, mt=mt, hds=hds: sy.dma_start(
                    out=s1_r[:, mt, :], in_=s1sb[:, mt % 2, :]).then_inc(hds, 16))
            p.wait("sync", "n_ds0")
            p.wait("sync", "n_ds1")




def build_fused(n_experts=MOE_E, nch=32):
    nc = bass.Bass("TRN2", target_bir_lowering=False)
    ctx = ExitStack()
    nc._ctx_keepalive = ctx
    dd0 = _l0_drams(nc, x1_kind="Internal")
    dd1 = _l1_drams(nc, x1_kind=None)
    dd1["x1"] = dd0["x1"]
    p = Prog(nc, ctx)
    t = {}
    with ExitStack() as l0ctx:
        emit_l0_body(p, l0ctx, nc, t, dd0, nch=nch, perm_ctx=ctx)
        p.barrier()
    p.release_sems(keep=("v_ini", "g_ini", "d_w", "c_dw", "f0_dw0", "f0_dw1"))
    p.fresh_names = {"n_dw0", "n_dw1", "n_dwo", "n_drk0", "n_drk1",
                     "m_dgw", "m_dw0", "m_dw1"}
    with ExitStack() as l1ctx:
        emit_l1_body(p, l1ctx, nc, t, dd1, n_experts=n_experts)
    with nc.Block() as block:
        p.build(block)
    return nc


def _prep_l0_inputs(inputs):
    x = np.asarray(inputs["L0_attn_x"], dtype=np.float32).reshape(NTOK, D)
    w_attnT = np.ascontiguousarray(np.asarray(inputs["L0_attn_w_attn"], np.float32).T)
    w_projT = np.ascontiguousarray(np.asarray(inputs["L0_attn_w_proj"], np.float32).T)
    wg0T = np.ascontiguousarray(np.asarray(inputs["L0_ffn_w_gate"], np.int32).T)
    wu0T = np.ascontiguousarray(np.asarray(inputs["L0_ffn_w_up"], np.int32).T)
    wd0T = np.ascontiguousarray(np.asarray(inputs["L0_ffn_w_down"], np.int32).T)
    xr = np.asarray(inputs["L0_attn_x"], np.float32)
    maps = []
    for c in range(NCORES):
        xo = np.ascontiguousarray(xr[:, COLS * c:COLS * (c + 1), :]).reshape(TOK, D)
        maps.append({
            "x_full": x, "x_own": xo,
            "thr": np.array([[COLS * c]], dtype=np.float32),
            "w_attnT": w_attnT, "w_projT": w_projT,
            "wg0T": wg0T, "wu0T": wu0T, "wd0T": wd0T,
        })
    return maps




def _prep_l1_inputs(inputs, x1_full=None):
    f = np.float32
    wqT = np.ascontiguousarray(np.asarray(inputs["L1_attn_w_q"], f).T)
    wkT = np.ascontiguousarray(np.asarray(inputs["L1_attn_w_k"], f).T)
    wvT = np.ascontiguousarray(np.asarray(inputs["L1_attn_w_v"], f).T)
    woT = np.ascontiguousarray(np.asarray(inputs["L1_attn_w_o"], f).T)
    b_o = np.asarray(inputs["L1_attn_b_o"], f).reshape(1, D)
    rel_k = np.asarray(inputs["L1_attn_rel_k"], f)
    rel_v = np.asarray(inputs["L1_attn_rel_v"], f)
    pos = np.arange(B)
    idx = np.clip(pos[:, None] - pos[None, :], -128, 128) + 128
    rk_t = np.ascontiguousarray(np.tile(rel_k[idx], (1, 1, HEADS)))
    rv_t = np.ascontiguousarray(np.tile(rel_v[idx], (1, 1, HEADS)))
    gate_wT = np.ascontiguousarray(np.asarray(inputs["L1_ffn_gate_w"], f).T)
    gate_bias = np.asarray(inputs["L1_ffn_gate_bias"], f).reshape(1, 64)
    WgT = np.ascontiguousarray(np.asarray(inputs["L1_ffn_Wg"], f).transpose(0, 2, 1))
    WuT = np.ascontiguousarray(np.asarray(inputs["L1_ffn_Wu"], f).transpose(0, 2, 1))
    WdT = np.ascontiguousarray(np.asarray(inputs["L1_ffn_Wd"], f).transpose(0, 2, 1))
    SgT = np.ascontiguousarray(np.asarray(inputs["L1_ffn_Sg"], f).T)
    SuT = np.ascontiguousarray(np.asarray(inputs["L1_ffn_Su"], f).T)
    SdT = np.ascontiguousarray(np.asarray(inputs["L1_ffn_Sd"], f).T)
    maps = []
    for c in range(NCORES):
        m = {
            "wqT": wqT, "wkT": wkT, "wvT": wvT, "woT": woT,
            "b_o": b_o, "rk_t": rk_t, "rv_t": rv_t,
            "gate_wT": gate_wT, "gate_bias": gate_bias,
            "WgT": WgT, "WuT": WuT, "WdT": WdT,
            "SgT": SgT, "SuT": SuT, "SdT": SdT,
        }
        if x1_full is not None:
            m["x1_own"] = np.ascontiguousarray(
                x1_full[:, COLS * c:COLS * (c + 1), :]).reshape(TOK, D)
        maps.append(m)
    return maps


def kernel(**inputs):
    nc = build_fused()
    maps0 = _prep_l0_inputs(inputs)
    maps1 = _prep_l1_inputs(inputs, None)
    maps = [dict(m0, **m1) for m0, m1 in zip(maps0, maps1)]
    res = run_bass_kernel_spmd(nc, maps, core_ids=list(range(NCORES)))
    y = np.concatenate(
        [np.asarray(r["y_own"]).reshape(B, COLS, D) for r in res.results], axis=1)
    return y

